# version 01
# baseline (speedup 1.0000x reference)
import numpy as np
from contextlib import ExitStack
import concourse.bass as bass
import concourse.mybir as mybir

F32 = mybir.dt.float32
BF16 = mybir.dt.bfloat16
AF = mybir.ActivationFunctionType
ALU = mybir.AluOpType
AX = mybir.AxisListType

_STAGES = [16] * 7 + [8, 4, 2, 1]


def _f32(x):
    return float(np.float32(x))


class SE:

    def __init__(self, eng, sem, self_chain=True):
        self.e = eng
        self.sem = sem
        self.cnt = 0
        self.self_chain = self_chain

    def w(self, sem_cnt):
        sem, val = sem_cnt
        if val > 0:
            self.e.wait_ge(sem, val)

    def mark(self):
        return (self.sem, self.cnt)

    def __getattr__(self, name):
        fn = getattr(self.e, name)
        if not callable(fn):
            return fn

        def wrapped(*a, **k):
            inst = fn(*a, **k)
            if self.self_chain and self.cnt > 0:
                inst._wait_ge(self.sem, self.cnt)
            self.cnt += 1
            inst.then_inc(self.sem, 1)
            return inst

        return wrapped


QUANT_VEC_OPS = 1 + 1 + 1 + 4 * len(_STAGES) + 1 + 1 + 4 + 2 + 1 + 2 + 1 + 1


def emit_quant_vec(V, qb, out_ap):
    aa, sg = qb["aa"], qb["sg"]
    amax, rr, ss, mm, tt = qb["amax"], qb["rr"], qb["ss"], qb["mm"], qb["tt"]
    xh, acc, bb, cc, qq = qb["xh"], qb["acc"], qb["bb"], qb["cc"], qb["qq"]
    NB = 16
    c0 = V.cnt
    V.tensor_reduce(out=amax[:], in_=aa[:].rearrange("p (a b) -> p a b", b=32),
                    axis=AX.X, op=ALU.max)
    V.tensor_scalar(out=mm[:], in0=amax[:], scalar1=_f32(1e-30),
                    scalar2=_f32(2.0 ** 100), op0=ALU.max, op1=ALU.mult)
    V.memset(rr[:], _f32(2.0 ** 102))
    for E in _STAGES:
        V.tensor_scalar(out=tt[:], in0=mm[:], scalar1=_f32(2.0 ** E),
                        scalar2=None, op0=ALU.is_lt)
        V.tensor_scalar(out=tt[:], in0=tt[:], scalar1=_f32(1.0 - 2.0 ** (-E)),
                        scalar2=_f32(2.0 ** (-E)), op0=ALU.mult, op1=ALU.add)
        V.tensor_tensor(out=mm[:], in0=mm[:], in1=tt[:], op=ALU.mult)
        V.tensor_tensor(out=rr[:], in0=rr[:], in1=tt[:], op=ALU.mult)
    V.reciprocal(out=ss[:], in_=rr[:])
    rbc = rr[:].unsqueeze(2).broadcast_to([128, NB, 32])
    V.tensor_tensor(out=xh[:].rearrange("p (a b) -> p a b", b=32),
                    in0=aa[:].rearrange("p (a b) -> p a b", b=32),
                    in1=rbc, op=ALU.mult)
    V.tensor_scalar(out=acc[:], in0=xh[:], scalar1=0.25, scalar2=None, op0=ALU.is_ge)
    for m in (0.75, 1.25, 1.75):
        V.scalar_tensor_tensor(out=acc[:], in0=xh[:], scalar=m, in1=acc[:],
                               op0=ALU.is_ge, op1=ALU.add)
    V.tensor_scalar(out=bb[:], in0=xh[:], scalar1=2.5, scalar2=None, op0=ALU.is_ge)
    V.scalar_tensor_tensor(out=bb[:], in0=xh[:], scalar=3.5, in1=bb[:],
                           op0=ALU.is_ge, op1=ALU.add)
    V.tensor_scalar(out=cc[:], in0=xh[:], scalar1=5.0, scalar2=None, op0=ALU.is_ge)
    V.scalar_tensor_tensor(out=qq[:], in0=acc[:], scalar=0.5, in1=bb[:],
                           op0=ALU.mult, op1=ALU.add)
    V.scalar_tensor_tensor(out=qq[:], in0=cc[:], scalar=2.0, in1=qq[:],
                           op0=ALU.mult, op1=ALU.add)
    sbc = ss[:].unsqueeze(2).broadcast_to([128, NB, 32])
    V.tensor_tensor(out=qq[:].rearrange("p (a b) -> p a b", b=32),
                    in0=qq[:].rearrange("p (a b) -> p a b", b=32),
                    in1=sbc, op=ALU.mult)
    V.tensor_tensor(out=out_ap, in0=qq[:], in1=sg[:], op=ALU.mult)
    assert V.cnt - c0 == QUANT_VEC_OPS, (V.cnt - c0, QUANT_VEC_OPS)


def quant_bufs(nc, tag=""):
    sb = nc.alloc_sbuf_tensor
    return {
        "aa": sb("q_aa" + tag, [128, 512], F32),
        "sg": sb("q_sg" + tag, [128, 512], F32),
        "amax": sb("q_amax" + tag, [128, 16], F32),
        "rr": sb("q_rr" + tag, [128, 16], F32),
        "ss": sb("q_ss" + tag, [128, 16], F32),
        "mm": sb("q_mm" + tag, [128, 16], F32),
        "tt": sb("q_tt" + tag, [128, 16], F32),
        "xh": sb("q_xh" + tag, [128, 512], F32),
        "acc": sb("q_acc" + tag, [128, 512], F32),
        "bb": sb("q_bb" + tag, [128, 512], F32),
        "cc": sb("q_cc" + tag, [128, 512], F32),
        "qq": sb("q_qq" + tag, [128, 512], F32),
    }



def build_dequant(shapes):
    nc = bass.Bass("TRN2", target_bir_lowering=False)
    ident_d = nc.dram_tensor("ident", [128, 128], F32, kind="ExternalInput").ap()
    ins, outs, strips = {}, {}, []
    for name, R, C in shapes:
        assert R % 128 == 0 and C % 512 == 0
        ins[name] = nc.dram_tensor(name, [R, C], F32, kind="ExternalInput").ap()
        outs[name] = nc.dram_tensor(name + "_dq", [R, C], BF16, kind="ExternalOutput").ap()
        for rt in range(R // 128):
            for cs in range(C // 512):
                strips.append((name, rt, cs))
    nst = len(strips)
    VC = QUANT_VEC_OPS + 1
    SC = 3
    TC = 8

    sb = nc.alloc_sbuf_tensor
    pst = nc.alloc_psum_tensor
    es = ExitStack()
    sem = lambda n: es.enter_context(nc.semaphore(n))

    ident_s = sb("ident_s", [128, 128], F32)
    identb = sb("identb", [128, 128], BF16)
    ldb = [sb("ld0", [128, 512], F32), sb("ld1", [128, 512], F32)]
    wt = sb("wt", [128, 512], F32)
    dqt = sb("dqt", [128, 512], BF16)
    stb = [sb("st0", [128, 512], BF16), sb("st1", [128, 512], BF16)]
    qb = quant_bufs(nc)
    psA = pst("psA", [128, 512], F32)
    psB = pst("psB", [128, 512], BF16)

    smV, smS, smT = sem("smV"), sem("smS"), sem("smT")
    idm = sem("idm")
    lds = [sem("lda"), sem("ldb")]
    sts = [sem("sta"), sem("stb")]

    def v_after_dqt(i):
        return 1 + i * VC + QUANT_VEC_OPS

    def v_after_staging(i):
        return 1 + (i + 1) * VC

    with nc.Block() as block:
        @block.sync
        def _(sync):
            sync.dma_start(out=ident_s[:], in_=ident_d[:]).then_inc(idm, 16)
            for i, (name, rt, cs) in enumerate(strips):
                inst = sync.dma_start(
                    out=ldb[i % 2][:],
                    in_=ins[name][rt * 128:(rt + 1) * 128, cs * 512:(cs + 1) * 512])
                if i >= 2:
                    inst._wait_ge(smT, TC * (i - 2) + 4)
                inst.then_inc(lds[i % 2], 16)

        @block.gpsimd
        def _(g):
            for i, (name, rt, cs) in enumerate(strips):
                g.wait_ge(smV, v_after_staging(i))
                g.dma_start(out=outs[name][rt * 128:(rt + 1) * 128, cs * 512:(cs + 1) * 512],
                            in_=stb[i % 2][:]).then_inc(sts[i % 2], 16)
            for par in range(2):
                n = len([1 for i in range(nst) if i % 2 == par])
                if n:
                    g.wait_ge(sts[par], 16 * n)

        @block.scalar
        def _(s):
            S = SE(s, smS)
            for i in range(nst):
                S.w((smT, TC * i + 4))
                if i > 0:
                    S.w((smV, v_after_dqt(i - 1)))
                S.copy(out=wt[:], in_=psA[:])
                S.activation(out=qb["aa"][:], in_=wt[:], func=AF.Abs)
                S.activation(out=qb["sg"][:], in_=wt[:], func=AF.Sign)
            assert S.cnt == SC * nst

        @block.vector
        def _(v):
            V = SE(v, smV)
            V.w((idm, 16))
            V.tensor_copy(out=identb[:], in_=ident_s[:])
            for i in range(nst):
                V.w((smS, SC * i + SC))
                emit_quant_vec(V, qb, dqt[:])
                V.w((smT, TC * i + TC))
                if i >= 2:
                    V.w((sts[i % 2], 16 * (i // 2)))
                V.tensor_copy(out=stb[i % 2][:], in_=psB[:])
            assert V.cnt == 1 + VC * nst

        @block.tensor
        def _(t):
            T = SE(t, smT, self_chain=False)
            T.w((idm, 16))
            T.w((smV, 1))
            for i in range(nst):
                T.w((lds[i % 2], 16 * (i // 2 + 1)))
                if i > 0:
                    T.w((smS, SC * (i - 1) + 1))
                buf = ldb[i % 2]
                for kk in range(4):
                    T.transpose(psA[:, kk * 128:(kk + 1) * 128],
                                buf[:, kk * 128:(kk + 1) * 128], ident_s[:])
                T.w((smV, v_after_dqt(i)))
                if i > 0:
                    T.w((smV, v_after_staging(i - 1)))
                for kk in range(4):
                    T.transpose(psB[:, kk * 128:(kk + 1) * 128],
                                dqt[:, kk * 128:(kk + 1) * 128], identb[:])
            assert T.cnt == TC * nst

    return nc, [n for n, _, _ in shapes]


def ref_dequant(w):
    FP4_GRID = np.array([0.0, 0.5, 1.0, 1.5, 2.0, 3.0, 4.0, 6.0], dtype=w.dtype)
    FP4_MIDS = np.array([0.25, 0.75, 1.25, 1.75, 2.5, 3.5, 5.0], dtype=w.dtype)
    K, N = w.shape
    wb = w.reshape(K // 32, 32, N)
    amax = np.max(np.abs(wb), axis=1, keepdims=True)
    expo = np.floor(np.log2(np.maximum(amax, 1e-30))) - 2.0
    scales = np.exp2(expo)
    x = wb / scales
    idx = np.digitize(np.abs(x), FP4_MIDS)
    q = np.sign(x) * FP4_GRID[idx]
    return (q * scales).reshape(K, N)




class PEng:
    def __init__(self, name, sems, eng=None, live=False, self_chain=True):
        self.name, self.sems, self.e = name, sems, eng
        self.live, self.self_chain, self.cnt = live, self_chain, 0

    def mark(self):
        return (self.name, self.cnt)

    def w(self, dep):
        if dep is None:
            return
        nm, val = dep
        if nm == self.name or val <= 0:
            return
        if self.live:
            self.e.wait_ge(self.sems[nm], val)

    def __getattr__(self, op):
        def f(*a, **k):
            self.cnt += 1
            if self.live:
                inst = getattr(self.e, op)(*a, **k)
                if self.self_chain and self.cnt > 1:
                    inst._wait_ge(self.sems[self.name], self.cnt - 1)
                inst.then_inc(self.sems[self.name], 1)
                return inst
            return None
        return f


class PDma:
    def __init__(self, name, sems, eng=None, live=False):
        self.name, self.sems, self.e = name, sems, eng
        self.live, self.cnt = live, 0

    def dma(self, out=None, in_=None, wait=None):
        self.cnt += 1
        if self.live:
            if self.cnt > 1:
                self.e.wait_ge(self.sems[self.name], 16 * (self.cnt - 1))
            inst = self.e.dma_start(out=out, in_=in_)
            if wait is not None:
                nm, val = wait
                if val > 0:
                    inst._wait_ge(self.sems[nm], val)
            inst.then_inc(self.sems[self.name], 16)
        return (self.name, self.cnt * 16)

    def mark(self):
        return (self.name, self.cnt * 16)

    def w(self, dep):
        if dep is None:
            return
        nm, val = dep
        if nm == self.name or val <= 0:
            return
        if self.live:
            self.e.wait_ge(self.sems[nm], val)

    def drain(self):
        if self.live and self.cnt:
            self.e.wait_ge(self.sems[self.name], self.cnt * 16)


def run_passes(nc, sems, prog):
    with nc.Block() as block:
        @block.sync
        def _(sync):
            prog(PEng("T", sems), PEng("V", sems), PEng("S", sems),
                 PDma("SY", sems, sync, True), PDma("GP", sems))

        @block.gpsimd
        def _(g):
            prog(PEng("T", sems), PEng("V", sems), PEng("S", sems),
                 PDma("SY", sems), PDma("GP", sems, g, True))

        @block.scalar
        def _(s):
            prog(PEng("T", sems), PEng("V", sems), PEng("S", sems, s, True),
                 PDma("SY", sems), PDma("GP", sems))

        @block.vector
        def _(v):
            prog(PEng("T", sems), PEng("V", sems, v, True), PEng("S", sems),
                 PDma("SY", sems), PDma("GP", sems))

        @block.tensor
        def _(t):
            prog(PEng("T", sems, t, True, self_chain=False), PEng("V", sems),
                 PEng("S", sems), PDma("SY", sems), PDma("GP", sems))



def build_attn(n_tok, dbg_at=None):
    assert n_tok % 512 == 0
    NS = n_tok // 512
    nc = bass.Bass("TRN2", target_bir_lowering=False)
    dr = lambda n, s, dt=F32, k="ExternalInput": nc.dram_tensor(n, s, dt, kind=k).ap()
    xb = dr("xb", [n_tok, 2048])
    Wq_s = dr("Wq_s", [512, 2048])
    Wk_s = dr("Wk_s", [512, 2048])
    Wv_s = dr("Wv_s", [512, 2048])
    Wbeta_s = dr("Wbeta_s", [4, 2048])
    Wg1 = dr("Wg1", [128, 2048])
    Wg2_s = dr("Wg2_s", [512, 128])
    gnorm_cols = dr("gnorm_cols", [128, 4])
    ident_d = dr("ident", [128, 128])
    mask_ln_d = dr("mask_lowneg", [128, 128])
    mask_tu_d = dr("mask_triu", [128, 128], BF16)
    og_out = dr("og_out", [512, n_tok], BF16, k="ExternalOutput")
    dbg_out = dr("dbg_out", [128, 512], F32, k="ExternalOutput") if dbg_at is not None else None

    sb = nc.alloc_sbuf_tensor
    pst = nc.alloc_psum_tensor
    es = ExitStack()

    ident_s = sb("ident_s", [128, 128], F32)
    identb = sb("identb", [128, 128], BF16)
    mask_ln = sb("mask_ln", [128, 128], F32)
    mask_tu = sb("mask_tu", [128, 128], BF16)
    gnc = sb("gnc", [128, 4], F32)
    wqT = sb("wqT", [128, 16 * 512], BF16)
    wkT = sb("wkT", [128, 16 * 512], BF16)
    wvT = sb("wvT", [128, 16 * 512], BF16)
    wbT = sb("wbT", [128, 16 * 4], BF16)
    wg1T = sb("wg1T", [128, 16 * 128], BF16)
    wg2T = sb("wg2T", [128, 4 * 128], BF16)
    wld = [sb("wld0", [128, 2048], F32), sb("wld1", [128, 2048], F32)]
    x_tm = [sb("x_tm0", [128, 2048], F32), sb("x_tm1", [128, 2048], F32)]
    x_cm = sb("x_cm", [128, 16 * 512], BF16)
    q_h = [sb(f"q_{h}", [128, 512], BF16) for h in range(4)]
    kr_h = [sb(f"kr_{h}", [128, 512], BF16) for h in range(4)]
    k_h = [sb(f"k_{h}", [128, 512], BF16) for h in range(4)]
    v_h = [sb(f"v_{h}", [128, 512], BF16) for h in range(4)]
    ksq = sb("ksq", [128, 512], F32)
    xg1 = sb("xg1", [128, 512], BF16)
    beta_sb = sb("beta_sb", [4, 512], F32)
    rowb = [sb(f"rowb{i}", [1, 512], F32) for i in range(3)]
    ones_col = sb("ones_col", [128, 1], F32)
    ones_row = sb("ones_row", [1, 128], F32)
    Ablk = sb("Ablk", [32, 10 * 32], F32)
    Ablk_bf = sb("Ablk_bf", [32, 6 * 32], BF16)
    W32 = [sb(f"W32_{j}", [32, 32], F32) for j in range(5)]
    Z32 = [sb(f"Z32_{j}", [32, 32], F32) for j in range(5)]
    Gg = [sb("Gg_a", [32, 32], F32), sb("Gg_c", [32, 32], F32)]
    PTb4 = sb("PTb4", [32, 4 * 128], BF16)
    Rpre = sb("Rpre", [128, 128], BF16)
    rb4 = sb("rb4", [32, 4 * 128], F32)
    Ub4 = sb("Ub4", [32, 4 * 128], BF16)
    Ktb4 = sb("Ktb4", [32, 4 * 128], BF16)
    betaT4 = sb("betaT4", [32, 16 * 4], F32)
    nbetaT4 = sb("nbetaT4", [32, 16 * 4], F32)
    S_f = [sb(f"Sf_{h}", [128, 128], F32) for h in range(4)]
    S_b = [sb(f"Sb_{h}", [128, 128], BF16) for h in range(4)]
    ogf = sb("ogf", [128, 128], F32)
    ogsq = sb("ogsq", [128, 128], F32)
    gate = sb("gate", [128, 128], BF16)
    og_st = [sb(f"og_st{h}", [128, 512], BF16) for h in range(4)]
    rb_s = sb("rb_s", [1, 128], F32)
    rb_t = sb("rb_t", [1, 128], F32)
    rb_u = sb("rb_u", [1, 128], F32)
    rb_v = sb("rb_v", [1, 128], F32)

    psX = pst("psX", [128, 512], F32)
    psQ = pst("psQ", [128, 512], F32)
    psC = pst("psC", [128, 512], F32)
    psR = pst("psR", [128, 512], F32)
    psBt = pst("psBt", [128, 1024], BF16)
    psRow = pst("psRow", [128, 512], F32)

    sems = {}
    for nm in ("T", "V", "S", "SY", "GP"):
        sems[nm] = es.enter_context(nc.semaphore("sem" + nm))

    EPS = _f32(1e-6)

    def prog(T, V, S, SY, GP):
        SY.dma(ident_s[:], ident_d)
        SY.dma(mask_ln[:], mask_ln_d)
        SY.dma(mask_tu[:], mask_tu_d)
        m_cst = SY.dma(gnc[:], gnorm_cols)
        V.w(m_cst)
        V.tensor_copy(out=identb[:], in_=ident_s[:])
        V.memset(ones_col[:], 1.0)
        V.memset(ones_row[:], 1.0)
        for h in range(4):
            V.memset(S_f[h][:], 0.0)
            V.memset(S_b[h][:], 0.0)

        loads = []
        for mat in range(3):
            for m in range(4):
                loads.append(("qkv", mat, m))
        loads.append(("wg1", 0, 0))
        loads.append(("wbeta", 0, 0))
        for t4 in range(4):
            loads.append(("wg2", 0, t4))
        tmark_after_load = [None] * len(loads)
        vmark_after_load = [None] * len(loads)
        ld_marks = []
        for i, (kind, mat, m) in enumerate(loads):
            src = None
            if kind == "qkv":
                src = (Wq_s, Wk_s, Wv_s)[mat][m * 128:(m + 1) * 128, :]
                dst = wld[i % 2][:, :]
            elif kind == "wg1":
                src = Wg1
                dst = wld[i % 2][:, :]
            elif kind == "wbeta":
                src = Wbeta_s
                dst = wld[i % 2][0:4, :]
            else:
                src = Wg2_s[m * 128:(m + 1) * 128, :]
                dst = wld[i % 2][:, 0:128]
            wait = tmark_after_load[i - 2] if i >= 2 else None
            ld_marks.append(SY.dma(dst, src, wait=wait))

            T.w(ld_marks[i])
            T.w(V.mark())
            if kind == "qkv":
                dstT = (wqT, wkT, wvT)[mat]
                for g in range(4):
                    if g > 0:
                        T.w(V.mark())
                    for kk4 in range(4):
                        kk = g * 4 + kk4
                        T.transpose(psX[:, kk4 * 128:(kk4 + 1) * 128],
                                    wld[i % 2][:, kk * 128:(kk + 1) * 128], ident_s[:])
                    mt = T.mark()
                    V.w(mt)
                    for kk4 in range(4):
                        kk = g * 4 + kk4
                        V.tensor_copy(out=dstT[:, kk * 512 + m * 128: kk * 512 + m * 128 + 128],
                                      in_=psX[:, kk4 * 128:(kk4 + 1) * 128])
            elif kind == "wg1":
                for g in range(4):
                    if g > 0:
                        T.w(V.mark())
                    for kk4 in range(4):
                        kk = g * 4 + kk4
                        T.transpose(psX[:, kk4 * 128:(kk4 + 1) * 128],
                                    wld[i % 2][:, kk * 128:(kk + 1) * 128], ident_s[:])
                    mt = T.mark()
                    V.w(mt)
                    for kk4 in range(4):
                        kk = g * 4 + kk4
                        V.tensor_copy(out=wg1T[:, kk * 128:(kk + 1) * 128],
                                      in_=psX[:, kk4 * 128:(kk4 + 1) * 128])
            elif kind == "wbeta":
                for g in range(4):
                    if g > 0:
                        T.w(V.mark())
                    for kk4 in range(4):
                        kk = g * 4 + kk4
                        T.transpose(psX[:, kk4 * 128:kk4 * 128 + 4],
                                    wld[i % 2][0:4, kk * 128:(kk + 1) * 128], ident_s[0:4, 0:4])
                    mt = T.mark()
                    V.w(mt)
                    for kk4 in range(4):
                        kk = g * 4 + kk4
                        V.tensor_copy(out=wbT[:, kk * 4:(kk + 1) * 4],
                                      in_=psX[:, kk4 * 128:kk4 * 128 + 4])
            else:
                T.transpose(psX[:, 0:128], wld[i % 2][:, 0:128], ident_s[:])
                mt = T.mark()
                V.w(mt)
                V.tensor_copy(out=wg2T[:, m * 128:(m + 1) * 128], in_=psX[:, 0:128])
            tmark_after_load[i] = T.mark()
            vmark_after_load[i] = V.mark()

        xld_marks = {}
        t_after_x = {}
        for si in range(NS):
            for tt in range(4):
                gi = si * 4 + tt
                wait = t_after_x.get(gi - 2)
                xld_marks[gi] = SY.dma(x_tm[gi % 2][:],
                                       xb[gi * 128:(gi + 1) * 128, :], wait=wait)
                T.w(xld_marks[gi])
                T.w(V.mark())
                for g in range(4):
                    if g > 0:
                        T.w(V.mark())
                    for kk4 in range(4):
                        kk = g * 4 + kk4
                        T.transpose(psX[:, kk4 * 128:(kk4 + 1) * 128],
                                    x_tm[gi % 2][:, kk * 128:(kk + 1) * 128], ident_s[:])
                    mt = T.mark()
                    V.w(mt)
                    for kk4 in range(4):
                        kk = g * 4 + kk4
                        V.tensor_copy(out=x_cm[:, kk * 512 + tt * 128: kk * 512 + tt * 128 + 128],
                                      in_=psX[:, kk4 * 128:(kk4 + 1) * 128])
                t_after_x[gi] = T.mark()

            m_xcm = V.mark()

            T.w(m_xcm)
            for kk in range(16):
                T.matmul(psRow[0:4, :], wbT[:, kk * 4:(kk + 1) * 4],
                         x_cm[:, kk * 512:(kk + 1) * 512],
                         start=(kk == 0), stop=(kk == 15))
            S.w(T.mark())
            S.activation(out=beta_sb[:], in_=psRow[0:4, :], func=AF.Sigmoid)
            V.w(S.mark())
            V.tensor_scalar(out=beta_sb[:], in0=beta_sb[:], scalar1=2.0,
                            scalar2=None, op0=ALU.mult)
            T.w(V.mark())
            for j16 in range(16):
                T.transpose(psX[0:32, j16 * 4:j16 * 4 + 4],
                            beta_sb[0:4, j16 * 32:(j16 + 1) * 32], ident_s[0:4, 0:4])
            V.w(T.mark())
            V.tensor_copy(out=betaT4[:, :], in_=psX[0:32, 0:64])
            V.tensor_scalar(out=nbetaT4[:, :], in0=betaT4[:, :], scalar1=-1.0,
                            scalar2=None, op0=ALU.mult)

            for h in range(4):
                for mat, (wT, dst) in enumerate(((wqT, q_h[h]), (wkT, kr_h[h]), (wvT, v_h[h]))):
                    T.w(V.mark())
                    T.w(S.mark())
                    for kk in range(16):
                        T.matmul(psQ[:, :], wT[:, kk * 512 + h * 128: kk * 512 + h * 128 + 128],
                                 x_cm[:, kk * 512:(kk + 1) * 512],
                                 start=(kk == 0), stop=(kk == 15))
                    mt = T.mark()
                    S.w(mt)
                    if mat == 0:
                        S.activation(out=dst[:], in_=psQ[:, :], func=AF.Silu)
                    elif mat == 1:
                        S.activation(out=dst[:], in_=psQ[:, :], func=AF.Silu)
                    else:
                        S.copy(out=dst[:], in_=psQ[:, :])
                V.w(S.mark())
                V.tensor_tensor(out=ksq[:], in0=kr_h[h][:], in1=kr_h[h][:], op=ALU.mult)
                T.w(V.mark())
                T.matmul(psRow[0:1, :], ones_col[:], ksq[:], start=True, stop=True)
                V.w(T.mark())
                V.tensor_scalar(out=rowb[0][:], in0=psRow[0:1, :], scalar1=EPS,
                                scalar2=None, op0=ALU.add)
                S.w(V.mark())
                S.activation(out=rowb[1][:], in_=rowb[0][:], func=AF.Sqrt)
                V.w(S.mark())
                V.reciprocal(out=rowb[2][:], in_=rowb[1][:])
                V.tensor_tensor(out=rowb[1][:], in0=rowb[2][:], in1=rowb[2][:], op=ALU.mult)
                V.tensor_tensor(out=rowb[1][:], in0=rowb[1][:], in1=rowb[0][:], op=ALU.mult)
                V.tensor_scalar(out=rowb[1][:], in0=rowb[1][:], scalar1=-0.5,
                                scalar2=1.5, op0=ALU.mult, op1=ALU.add)
                V.tensor_tensor(out=rowb[2][:], in0=rowb[2][:], in1=rowb[1][:], op=ALU.mult)
                T.w(V.mark())
                T.matmul(psX[:, 0:512], ones_row[:], rowb[2][:], start=True, stop=True)
                V.w(T.mark())
                V.tensor_tensor(out=k_h[h][:], in0=kr_h[h][:], in1=psX[:, :], op=ALU.mult)

            T.w(V.mark())
            for kk in range(16):
                T.matmul(psQ[:, :], wg1T[:, kk * 128:(kk + 1) * 128],
                         x_cm[:, kk * 512:(kk + 1) * 512],
                         start=(kk == 0), stop=(kk == 15))
            V.w(T.mark())
            V.tensor_copy(out=xg1[:], in_=psQ[:, :])

            for c_loc in range(4):
                for h in range(4):
                    cs = slice(c_loc * 128, (c_loc + 1) * 128)
                    Kc = k_h[h][:, cs]
                    Qc = q_h[h][:, cs]
                    Vc = v_h[h][:, cs]

                    def sb32(b):
                        return slice((c_loc * 4 + b) * 32, (c_loc * 4 + b) * 32 + 32)

                    def bcol(b, neg=False):
                        t = nbetaT4 if neg else betaT4
                        j = (c_loc * 4 + b) * 4 + h
                        return t[:, j:j + 1]

                    pairs = [(b, b) for b in range(4)] + \
                            [(b, b2) for b in range(4) for b2 in range(b + 1, 4)]
                    T.w(V.mark())
                    for idx, (b, b2) in enumerate(pairs):
                        T.matmul(psQ[0:32, idx * 32:(idx + 1) * 32],
                                 Kc[:, b * 32:(b + 1) * 32], Kc[:, b2 * 32:(b2 + 1) * 32],
                                 start=True, stop=True)
                    V.w(T.mark())
                    V.tensor_copy(out=Ablk[:, :], in_=psQ[0:32, 0:320])
                    V.tensor_copy(out=Ablk_bf[:, :], in_=psQ[0:32, 128:320])
                    T.w(V.mark())
                    T.matmul(psR[:, 0:128], S_b[h][:], Kc, start=True, stop=True)
                    V.w(T.mark())
                    V.scalar_tensor_tensor(out=Rpre[:], in0=psR[:, 0:128], scalar=-1.0,
                                           in1=Vc, op0=ALU.mult, op1=ALU.add)
                    T.w(V.mark())
                    for b in range(4):
                        T.transpose(psBt[0:32, b * 128:(b + 1) * 128],
                                    Rpre[:, b * 32:(b + 1) * 32], identb[:])
                        T.transpose(psBt[0:32, 512 + b * 128:512 + (b + 1) * 128],
                                    Kc[:, b * 32:(b + 1) * 32], identb[:])
                    V.w(T.mark())
                    for b in range(4):
                        V.tensor_scalar(out=rb4[:, b * 128:(b + 1) * 128],
                                        in0=psBt[0:32, b * 128:(b + 1) * 128],
                                        scalar1=bcol(b), scalar2=None, op0=ALU.mult)
                        V.tensor_copy(out=Ktb4[:, b * 128:(b + 1) * 128],
                                      in_=psBt[0:32, 512 + b * 128:512 + (b + 1) * 128])
                    T.w(V.mark())
                    for b in range(4):
                        T.matmul(psQ[0:32, 320 + 0:320 + 128] if False else
                                 psC[0:32, b * 128:(b + 1) * 128],
                                 Kc[:, b * 32:(b + 1) * 32], Qc, start=True, stop=True)
                    V.w(T.mark())
                    for b in range(4):
                        V.tensor_tensor(out=PTb4[:, b * 128:(b + 1) * 128],
                                        in0=psC[0:32, b * 128:(b + 1) * 128],
                                        in1=mask_tu[b * 32:b * 32 + 32, :], op=ALU.mult)
                    if dbg_at == ("PT", si, c_loc, h):
                        V.tensor_copy(out=x_tm[0][0:32, 0:512], in_=PTb4[:, 0:512])
                        SY.dma(dbg_out[0:32, :], x_tm[0][0:32, 0:512], wait=V.mark())
                    for b in range(4):
                        didx = b
                        V.scalar_tensor_tensor(out=W32[0][:], in0=Ablk[:, didx * 32:(didx + 1) * 32],
                                               scalar=bcol(b), in1=mask_ln[0:32, 0:32],
                                               op0=ALU.mult, op1=ALU.mult)
                        T.w(V.mark())
                        T.transpose(psC[0:32, 448:480], W32[0][:], ident_s[0:32, 0:32])
                        V.w(T.mark())
                        V.tensor_copy(out=Z32[0][:], in_=psC[0:32, 448:480])
                        for j in range(1, 5):
                            T.w(V.mark())
                            T.matmul(psC[0:32, 448:480], Z32[j - 1][:], W32[j - 1][:], start=True, stop=True)
                            T.matmul(psC[0:32, 480:512], W32[j - 1][:], Z32[j - 1][:], start=True, stop=True)
                            V.w(T.mark())
                            V.tensor_copy(out=W32[j][:], in_=psC[0:32, 448:480])
                            V.tensor_copy(out=Z32[j][:], in_=psC[0:32, 480:512])
                        V.tensor_tensor(out=Gg[0][:], in0=Z32[0][:], in1=ident_s[0:32, 0:32], op=ALU.add)
                        for j in range(1, 5):
                            T.w(V.mark())
                            T.matmul(psC[0:32, 448:480], W32[j][:], Gg[(j - 1) % 2][:], start=True, stop=True)
                            V.w(T.mark())
                            V.tensor_tensor(out=Gg[j % 2][:], in0=psC[0:32, 448:480],
                                            in1=Gg[(j - 1) % 2][:], op=ALU.add)
                        G4 = Gg[0]
                        T.w(V.mark())
                        T.matmul(psR[0:32, 128:256], G4[:], rb4[:, b * 128:(b + 1) * 128],
                                 start=True, stop=True)
                        V.w(T.mark())
                        V.tensor_copy(out=Ub4[:, b * 128:(b + 1) * 128], in_=psR[0:32, 128:256])
                        for b2 in range(b + 1, 4):
                            pidx = 4 + [(bb, bb2) for bb in range(4) for bb2 in range(bb + 1, 4)].index((b, b2))
                            T.w(V.mark())
                            T.matmul(psR[0:32, 256:384],
                                     Ablk_bf[:, (pidx - 4) * 32:(pidx - 3) * 32],
                                     Ub4[:, b * 128:(b + 1) * 128], start=True, stop=True)
                            V.w(T.mark())
                            V.scalar_tensor_tensor(out=rb4[:, b2 * 128:(b2 + 1) * 128],
                                                   in0=psR[0:32, 256:384], scalar=bcol(b2, neg=True),
                                                   in1=rb4[:, b2 * 128:(b2 + 1) * 128],
                                                   op0=ALU.mult, op1=ALU.add)
                    if dbg_at == ("PT2", si, c_loc, h):
                        V.tensor_copy(out=x_tm[0][0:32, 0:512], in_=PTb4[:, 0:512])
                        SY.dma(dbg_out[0:32, :], x_tm[0][0:32, 0:512], wait=V.mark())
                    T.w(V.mark())
                    T.matmul(psR[:, 384:512], S_b[h][:], Qc, start=True, stop=False)
                    for b in range(4):
                        T.matmul(psR[:, 384:512], Ub4[:, b * 128:(b + 1) * 128],
                                 PTb4[:, b * 128:(b + 1) * 128],
                                 start=False, stop=(b == 3))
                    if dbg_at == (si, c_loc, h):
                        V.w(T.mark())
                        V.tensor_copy(out=x_tm[0][:, 0:128], in_=psR[:, 0:128])
                        V.tensor_copy(out=x_tm[0][:, 384:512], in_=psR[:, 384:512])
                        mdbg = V.mark()
                        SY.dma(dbg_out, x_tm[0][:, 0:512], wait=mdbg)
                    for b in range(4):
                        T.matmul(psQ[:, 384:512], Ktb4[:, b * 128:(b + 1) * 128],
                                 Ub4[:, b * 128:(b + 1) * 128],
                                 start=(b == 0), stop=(b == 3))
                    V.w(T.mark())
                    V.tensor_tensor(out=S_f[h][:], in0=S_f[h][:], in1=psQ[:, 384:512], op=ALU.add)
                    V.tensor_copy(out=S_b[h][:], in_=S_f[h][:])
                    T.matmul(psRow[:, 0:128], wg2T[:, h * 128:(h + 1) * 128],
                             xg1[:, cs], start=True, stop=True)
                    S.w(T.mark())
                    S.activation(out=gate[:], in_=psRow[:, 0:128], func=AF.Sigmoid)
                    V.w(S.mark())
                    V.tensor_tensor(out=ogf[:], in0=psR[:, 384:512], in1=gate[:], op=ALU.mult)
                    V.tensor_tensor(out=ogsq[:], in0=ogf[:], in1=ogf[:], op=ALU.mult)
                    T.w(V.mark())
                    T.matmul(psRow[0:1, 128:256], ones_col[:], ogsq[:], start=True, stop=True)
                    V.w(T.mark())
                    V.tensor_scalar(out=rb_s[:], in0=psRow[0:1, 128:256],
                                    scalar1=_f32(1.0 / 128.0), scalar2=EPS,
                                    op0=ALU.mult, op1=ALU.add)
                    S.w(V.mark())
                    S.activation(out=rb_t[:], in_=rb_s[:], func=AF.Sqrt)
                    V.w(S.mark())
                    V.reciprocal(out=rb_u[:], in_=rb_t[:])
                    V.tensor_tensor(out=rb_v[:], in0=rb_u[:], in1=rb_u[:], op=ALU.mult)
                    V.tensor_tensor(out=rb_v[:], in0=rb_v[:], in1=rb_s[:], op=ALU.mult)
                    V.tensor_scalar(out=rb_v[:], in0=rb_v[:], scalar1=-0.5,
                                    scalar2=1.5, op0=ALU.mult, op1=ALU.add)
                    V.tensor_tensor(out=rb_u[:], in0=rb_u[:], in1=rb_v[:], op=ALU.mult)
                    T.w(V.mark())
                    T.matmul(psRow[:, 256:384], ones_row[:], rb_u[0:1, :], start=True, stop=True)
                    V.w(T.mark())
                    V.tensor_tensor(out=ogf[:], in0=ogf[:], in1=psRow[:, 256:384], op=ALU.mult)
                    if si > 0 and c_loc == 0 and h == 0:
                        V.w(("GP", 16 * 4 * si))
                    V.tensor_scalar(out=og_st[h][:, cs], in0=ogf[:],
                                    scalar1=gnc[:, h:h + 1], scalar2=None, op0=ALU.mult)

            mv = V.mark()
            for h in range(4):
                GP.dma(og_out[h * 128:(h + 1) * 128, si * 512:(si + 1) * 512],
                       og_st[h][:], wait=mv)
        GP.drain()
        SY.drain()

    run_passes(nc, sems, prog)
    return nc



def build_tail(n_tok):
    assert n_tok % 512 == 0
    NH = n_tok // 512
    nc = bass.Bass("TRN2", target_bir_lowering=False)
    dr = lambda n, s, dt=F32, k="ExternalInput": nc.dram_tensor(n, s, dt, kind=k).ap()
    og_d = dr("og_full", [2048, n_tok], BF16)
    x_d = dr("x_slice", [n_tok, 2048])
    Wo_d = dr("Wo", [2048, 2048])
    up_d = dr("up_dq", [2048, 8192], BF16)
    dn_d = dr("dn_dq", [8192, 2048], BF16)
    ub_d = dr("ub_cols", [128, 64])
    db_d = dr("db_cols", [128, 16])
    ident_d = dr("ident", [128, 128])
    y_d = dr("y_out", [n_tok, 2048], F32, k="ExternalOutput")

    sb = nc.alloc_sbuf_tensor
    pst = nc.alloc_psum_tensor
    es = ExitStack()

    ident_s = sb("ident_s", [128, 128], F32)
    ub = sb("ub", [128, 64], F32)
    db = sb("db", [128, 16], F32)
    _obase = nc.bump_sbuf(65536)[0]
    h_sb = nc.alloc_sbuf_tensor_at("h_sb", [128, 64 * 512], BF16, offset=_obase)
    x_cm = nc.alloc_sbuf_tensor_at("x_cm", [128, 16 * 512], F32, offset=_obase)
    og_sb = nc.alloc_sbuf_tensor_at("og_sb", [128, 16 * 512], BF16, offset=_obase + 32768)
    x_tm = [nc.alloc_sbuf_tensor_at("x_tm0", [128, 2048], F32, offset=_obase + 49152),
            nc.alloc_sbuf_tensor_at("x_tm1", [128, 2048], F32, offset=_obase + 57344)]
    wld = x_tm
    woT = sb("woT", [128, 16 * 128], BF16)
    y_f = sb("y_f", [128, 16 * 512], F32)
    _xbase = nc.bump_sbuf(16384)[0]
    x1b = nc.alloc_sbuf_tensor_at("x1b", [128, 16 * 512], BF16, offset=_xbase)
    outb = nc.alloc_sbuf_tensor_at("outb", [128, 2048], F32, offset=_xbase + 8192)
    sqb = sb("sqb", [128, 512], F32)
    upw = [sb("upw0", [128, 16 * 256], BF16), sb("upw1", [128, 16 * 256], BF16)]
    dnw = [sb("dnw0", [128, 64 * 128], BF16), sb("dnw1", [128, 64 * 128], BF16)]
    z_f = sb("z_f", [128, 16 * 512], F32)
    rowm = sb("rowm", [1, 512], F32)
    rowq = sb("rowq", [1, 512], F32)
    rowr = sb("rowr", [1, 512], F32)
    rowt = sb("rowt", [1, 512], F32)
    ones_col = sb("ones_col", [128, 1], F32)
    ones_row = sb("ones_row", [1, 128], F32)

    psA = pst("psA", [128, 512], F32)
    psB_ = pst("psB", [128, 512], F32)
    psT = pst("psT", [128, 512], F32)
    psRow = pst("psRow", [128, 512], F32)

    sems = {}
    for nm in ("T", "V", "S", "SY", "GP"):
        sems[nm] = es.enter_context(nc.semaphore("sem" + nm))

    def ln_rows(T, V, S, src_tiles, eps):
        T.w(V.mark())
        for k in range(16):
            T.matmul(psRow[0:1, 0:512], ones_col[:], src_tiles[k],
                     start=(k == 0), stop=(k == 15))
        V.w(T.mark())
        V.tensor_scalar(out=rowm[:], in0=psRow[0:1, 0:512], scalar1=_f32(1.0 / 2048.0),
                        scalar2=None, op0=ALU.mult)
        for k in range(16):
            V.tensor_tensor(out=sqb[:], in0=src_tiles[k], in1=src_tiles[k], op=ALU.mult)
            T.w(V.mark())
            T.matmul(psRow[0:1, 0:512], ones_col[:], sqb[:],
                     start=(k == 0), stop=(k == 15))
            if k < 15:
                V.w(T.mark())
        V.w(T.mark())
        V.tensor_scalar(out=rowq[:], in0=psRow[0:1, 0:512], scalar1=_f32(1.0 / 2048.0),
                        scalar2=None, op0=ALU.mult)
        V.tensor_tensor(out=rowt[:], in0=rowm[:], in1=rowm[:], op=ALU.mult)
        V.tensor_tensor(out=rowq[:], in0=rowq[:], in1=rowt[:], op=ALU.subtract)
        V.tensor_scalar(out=rowq[:], in0=rowq[:], scalar1=_f32(eps), scalar2=None, op0=ALU.add)
        S.w(V.mark())
        S.activation(out=rowt[:], in_=rowq[:], func=AF.Sqrt)
        V.w(S.mark())
        V.reciprocal(out=rowr[:], in_=rowt[:])
        V.tensor_tensor(out=rowt[:], in0=rowr[:], in1=rowr[:], op=ALU.mult)
        V.tensor_tensor(out=rowt[:], in0=rowt[:], in1=rowq[:], op=ALU.mult)
        V.tensor_scalar(out=rowt[:], in0=rowt[:], scalar1=-0.5, scalar2=1.5,
                        op0=ALU.mult, op1=ALU.add)
        V.tensor_tensor(out=rowr[:], in0=rowr[:], in1=rowt[:], op=ALU.mult)
        T.w(V.mark())
        T.matmul(psRow[:, 0:512], ones_row[:], rowm[0:1, 0:128], start=True, stop=False)
        T.matmul(psRow[:, 0:512], ones_row[:], rowm[0:1, 0:128], start=False, stop=True)

    EPS_LN = 1e-5

    def prog(T, V, S, SY, GP):
        m_c = SY.dma(ident_s[:], ident_d)
        SY.dma(ub[:], ub_d)
        m_c2 = SY.dma(db[:], db_d)
        V.w(m_c2)
        V.memset(ones_col[:], 1.0)
        V.memset(ones_row[:], 1.0)

        for half in range(NH):
            t0 = half * 512
            for tt in range(4):
                mld = SY.dma(x_tm[tt % 2][:], x_d[t0 + tt * 128: t0 + (tt + 1) * 128, :],
                             wait=T.mark())
                T.w(mld)
                T.w(V.mark())
                for g in range(4):
                    if g > 0:
                        T.w(V.mark())
                    for kk4 in range(4):
                        kk = g * 4 + kk4
                        T.transpose(psT[:, kk4 * 128:(kk4 + 1) * 128],
                                    x_tm[tt % 2][:, kk * 128:(kk + 1) * 128], ident_s[:])
                    V.w(T.mark())
                    for kk4 in range(4):
                        kk = g * 4 + kk4
                        V.tensor_copy(out=x_cm[:, kk * 512 + tt * 128: kk * 512 + tt * 128 + 128],
                                      in_=psT[:, kk4 * 128:(kk4 + 1) * 128])
            m_og = SY.dma(og_sb[:].rearrange("p (a c) -> p a c", a=16),
                          og_d[:, t0:t0 + 512].rearrange("(a p) c -> p a c", p=128))
            for m in range(16):
                mld = SY.dma(wld[m % 2][:], Wo_d[m * 128:(m + 1) * 128, :],
                             wait=T.mark())
                T.w(mld)
                T.w(V.mark())
                for g in range(4):
                    if g > 0:
                        T.w(V.mark())
                    for kk4 in range(4):
                        kk = g * 4 + kk4
                        T.transpose(psT[:, kk4 * 128:(kk4 + 1) * 128],
                                    wld[m % 2][:, kk * 128:(kk + 1) * 128], ident_s[:])
                    V.w(T.mark())
                    for kk4 in range(4):
                        kk = g * 4 + kk4
                        V.tensor_copy(out=woT[:, kk * 128:(kk + 1) * 128],
                                      in_=psT[:, kk4 * 128:(kk4 + 1) * 128])
                T.w(V.mark())
                T.w(m_og)
                for k in range(16):
                    T.matmul(psA[:, 0:512], woT[:, k * 128:(k + 1) * 128],
                             og_sb[:, k * 512:(k + 1) * 512],
                             start=(k == 0), stop=(k == 15))
                V.w(T.mark())
                V.tensor_tensor(out=y_f[:, m * 512:(m + 1) * 512], in0=psA[:, 0:512],
                                in1=x_cm[:, m * 512:(m + 1) * 512], op=ALU.add)
            ytiles = [y_f[:, k * 512:(k + 1) * 512] for k in range(16)]
            T.w(V.mark())
            for k in range(16):
                T.matmul(psRow[0:1, 0:512], ones_col[:], ytiles[k],
                         start=(k == 0), stop=(k == 15))
            V.w(T.mark())
            V.tensor_scalar(out=rowm[:], in0=psRow[0:1, 0:512],
                            scalar1=_f32(1.0 / 2048.0), scalar2=None, op0=ALU.mult)
            for k in range(16):
                V.tensor_tensor(out=sqb[:], in0=ytiles[k], in1=ytiles[k], op=ALU.mult)
                T.w(V.mark())
                T.matmul(psRow[0:1, 0:512], ones_col[:], sqb[:],
                         start=(k == 0), stop=(k == 15))
                V.w(T.mark())
            V.tensor_scalar(out=rowq[:], in0=psRow[0:1, 0:512],
                            scalar1=_f32(1.0 / 2048.0), scalar2=None, op0=ALU.mult)
            V.tensor_tensor(out=rowt[:], in0=rowm[:], in1=rowm[:], op=ALU.mult)
            V.tensor_tensor(out=rowq[:], in0=rowq[:], in1=rowt[:], op=ALU.subtract)
            V.tensor_scalar(out=rowq[:], in0=rowq[:], scalar1=_f32(EPS_LN),
                            scalar2=None, op0=ALU.add)
            S.w(V.mark())
            S.activation(out=rowt[:], in_=rowq[:], func=AF.Sqrt)
            V.w(S.mark())
            V.reciprocal(out=rowr[:], in_=rowt[:])
            V.tensor_tensor(out=rowt[:], in0=rowr[:], in1=rowr[:], op=ALU.mult)
            V.tensor_tensor(out=rowt[:], in0=rowt[:], in1=rowq[:], op=ALU.mult)
            V.tensor_scalar(out=rowt[:], in0=rowt[:], scalar1=-0.5, scalar2=1.5,
                            op0=ALU.mult, op1=ALU.add)
            V.tensor_tensor(out=rowr[:], in0=rowr[:], in1=rowt[:], op=ALU.mult)
            T.w(V.mark())
            T.matmul(psA[:, 0:512], ones_row[:], rowm[0:1, 0:512], start=True, stop=True)
            T.matmul(psB_[:, 0:512], ones_row[:], rowr[0:1, 0:512], start=True, stop=True)
            V.w(T.mark())
            for k in range(16):
                V.tensor_tensor(out=y_f[:, k * 512:(k + 1) * 512], in0=ytiles[k],
                                in1=psA[:, 0:512], op=ALU.subtract)
                V.tensor_tensor(out=y_f[:, k * 512:(k + 1) * 512], in0=ytiles[k],
                                in1=psB_[:, 0:512], op=ALU.mult)
                V.tensor_copy(out=x1b[:, k * 512:(k + 1) * 512], in_=ytiles[k])
            for mg in range(32):
                src = up_d[:, mg * 256:(mg + 1) * 256].rearrange("(a p) c -> p a c", p=128)
                mld = SY.dma(upw[mg % 2][:].rearrange("p (a c) -> p a c", a=16), src,
                             wait=T.mark())
                T.w(mld)
                T.w(S.mark())
                if mg == 0:
                    T.w(V.mark())
                for m2 in range(2):
                    m = mg * 2 + m2
                    for k in range(16):
                        T.matmul(psA[:, 0:512] if m2 == 0 else psB_[:, 0:512],
                                 upw[mg % 2][:, k * 256 + m2 * 128: k * 256 + m2 * 128 + 128],
                                 x1b[:, k * 512:(k + 1) * 512],
                                 start=(k == 0), stop=(k == 15))
                S.w(T.mark())
                for m2 in range(2):
                    m = mg * 2 + m2
                    S.activation(out=h_sb[:, m * 512:(m + 1) * 512],
                                 in_=psA[:, 0:512] if m2 == 0 else psB_[:, 0:512],
                                 func=AF.Gelu_apprx_tanh, bias=ub[:, m:m + 1])
            for m in range(16):
                src = dn_d[:, m * 128:(m + 1) * 128].rearrange("(a p) c -> p a c", p=128)
                mld = SY.dma(dnw[m % 2][:].rearrange("p (a c) -> p a c", a=64), src,
                             wait=T.mark())
                T.w(mld)
                T.w(S.mark())
                T.w(V.mark())
                for kf in range(64):
                    T.matmul(psA[:, 0:512], dnw[m % 2][:, kf * 128:(kf + 1) * 128],
                             h_sb[:, kf * 512:(kf + 1) * 512],
                             start=(kf == 0), stop=(kf == 63))
                V.w(T.mark())
                V.tensor_scalar(out=z_f[:, m * 512:(m + 1) * 512], in0=psA[:, 0:512],
                                scalar1=db[:, m:m + 1], scalar2=None, op0=ALU.add)
                V.tensor_tensor(out=z_f[:, m * 512:(m + 1) * 512],
                                in0=z_f[:, m * 512:(m + 1) * 512],
                                in1=y_f[:, m * 512:(m + 1) * 512], op=ALU.add)
            ztiles = [z_f[:, k * 512:(k + 1) * 512] for k in range(16)]
            T.w(V.mark())
            for k in range(16):
                T.matmul(psRow[0:1, 0:512], ones_col[:], ztiles[k],
                         start=(k == 0), stop=(k == 15))
            V.w(T.mark())
            V.tensor_scalar(out=rowm[:], in0=psRow[0:1, 0:512],
                            scalar1=_f32(1.0 / 2048.0), scalar2=None, op0=ALU.mult)
            for k in range(16):
                V.tensor_tensor(out=sqb[:], in0=ztiles[k], in1=ztiles[k], op=ALU.mult)
                T.w(V.mark())
                T.matmul(psRow[0:1, 0:512], ones_col[:], sqb[:],
                         start=(k == 0), stop=(k == 15))
                V.w(T.mark())
            V.tensor_scalar(out=rowq[:], in0=psRow[0:1, 0:512],
                            scalar1=_f32(1.0 / 2048.0), scalar2=None, op0=ALU.mult)
            V.tensor_tensor(out=rowt[:], in0=rowm[:], in1=rowm[:], op=ALU.mult)
            V.tensor_tensor(out=rowq[:], in0=rowq[:], in1=rowt[:], op=ALU.subtract)
            V.tensor_scalar(out=rowq[:], in0=rowq[:], scalar1=_f32(EPS_LN),
                            scalar2=None, op0=ALU.add)
            S.w(V.mark())
            S.activation(out=rowt[:], in_=rowq[:], func=AF.Sqrt)
            V.w(S.mark())
            V.reciprocal(out=rowr[:], in_=rowt[:])
            V.tensor_tensor(out=rowt[:], in0=rowr[:], in1=rowr[:], op=ALU.mult)
            V.tensor_tensor(out=rowt[:], in0=rowt[:], in1=rowq[:], op=ALU.mult)
            V.tensor_scalar(out=rowt[:], in0=rowt[:], scalar1=-0.5, scalar2=1.5,
                            op0=ALU.mult, op1=ALU.add)
            V.tensor_tensor(out=rowr[:], in0=rowr[:], in1=rowt[:], op=ALU.mult)
            T.w(V.mark())
            T.matmul(psA[:, 0:512], ones_row[:], rowm[0:1, 0:512], start=True, stop=True)
            T.matmul(psB_[:, 0:512], ones_row[:], rowr[0:1, 0:512], start=True, stop=True)
            V.w(T.mark())
            for k in range(16):
                V.tensor_tensor(out=z_f[:, k * 512:(k + 1) * 512], in0=ztiles[k],
                                in1=psA[:, 0:512], op=ALU.subtract)
                V.tensor_tensor(out=z_f[:, k * 512:(k + 1) * 512], in0=ztiles[k],
                                in1=psB_[:, 0:512], op=ALU.mult)
            for tt in range(4):
                T.w(V.mark())
                for k in range(16):
                    if k > 0 and k % 4 == 0:
                        T.w(V.mark())
                    T.transpose(psT[:, (k % 4) * 128:(k % 4 + 1) * 128],
                                z_f[:, k * 512 + tt * 128: k * 512 + tt * 128 + 128],
                                ident_s[:])
                    if k % 4 == 3:
                        V.w(T.mark())
                        for k4 in range(k - 3, k + 1):
                            V.tensor_copy(out=outb[:, k4 * 128:(k4 + 1) * 128],
                                          in_=psT[:, (k4 % 4) * 128:(k4 % 4 + 1) * 128])
                mo = V.mark()
                GP.dma(y_d[t0 + tt * 128: t0 + (tt + 1) * 128, :], outb[:], wait=mo)
                V.w(GP.mark())
        GP.drain()
        SY.drain()

    run_passes(nc, sems, prog)
    return nc


import ml_dtypes
from concourse.bass_utils import run_bass_kernel_spmd

_CACHE = {}


def _programs():
    if "dq" not in _CACHE:
        _CACHE["dq"] = build_dequant([("up", 256, 8192), ("dn", 1024, 2048)])[0]
        _CACHE["attn"] = build_attn(4096)
        _CACHE["tail"] = build_tail(1024)
    return _CACHE["dq"], _CACHE["attn"], _CACHE["tail"]


def kernel(L0_attn_x, L0_attn_Wq, L0_attn_Wk, L0_attn_Wv, L0_attn_Wbeta,
           L0_attn_Wg1, L0_attn_Wg2, L0_attn_g_norm, L0_attn_Wo,
           L0_ffn_up_kernel, L0_ffn_up_bias, L0_ffn_down_kernel,
           L0_ffn_down_bias):
    x = np.ascontiguousarray(np.asarray(L0_attn_x, dtype=np.float32))
    Wq = np.ascontiguousarray(np.asarray(L0_attn_Wq, dtype=np.float32))
    Wk = np.ascontiguousarray(np.asarray(L0_attn_Wk, dtype=np.float32))
    Wv = np.ascontiguousarray(np.asarray(L0_attn_Wv, dtype=np.float32))
    Wbeta = np.ascontiguousarray(np.asarray(L0_attn_Wbeta, dtype=np.float32))
    Wg1 = np.ascontiguousarray(np.asarray(L0_attn_Wg1, dtype=np.float32))
    Wg2 = np.ascontiguousarray(np.asarray(L0_attn_Wg2, dtype=np.float32))
    g_norm = np.ascontiguousarray(np.asarray(L0_attn_g_norm, dtype=np.float32))
    Wo = np.ascontiguousarray(np.asarray(L0_attn_Wo, dtype=np.float32))
    up = np.ascontiguousarray(np.asarray(L0_ffn_up_kernel, dtype=np.float32))
    ub = np.ascontiguousarray(np.asarray(L0_ffn_up_bias, dtype=np.float32))
    dn = np.ascontiguousarray(np.asarray(L0_ffn_down_kernel, dtype=np.float32))
    db = np.ascontiguousarray(np.asarray(L0_ffn_down_bias, dtype=np.float32))

    ident = np.eye(128, dtype=np.float32)
    ii, jj = np.indices((128, 128))
    mask_lowneg = np.where(ii > jj, -1.0, 0.0).astype(np.float32)
    mask_triu = np.where(ii <= jj, 1.0, 0.0).astype(ml_dtypes.bfloat16)

    nc_dq, nc_attn, nc_tail = _programs()

    in_maps = []
    for c in range(8):
        in_maps.append({
            "ident": ident,
            "up": np.ascontiguousarray(up[256 * c:256 * (c + 1)]),
            "dn": np.ascontiguousarray(dn[1024 * c:1024 * (c + 1)]),
        })
    res = run_bass_kernel_spmd(nc_dq, in_maps, core_ids=list(range(8)))
    up_dq = np.concatenate([np.asarray(res.results[c]["up_dq"]).view(ml_dtypes.bfloat16)
                            for c in range(8)], axis=0)
    dn_dq = np.concatenate([np.asarray(res.results[c]["dn_dq"]).view(ml_dtypes.bfloat16)
                            for c in range(8)], axis=0)

    in_maps = []
    for c in range(8):
        b, a = divmod(c, 4)
        rows = slice(512 * a, 512 * (a + 1))
        in_maps.append({
            "xb": np.ascontiguousarray(x[b]),
            "Wq_s": np.ascontiguousarray(Wq[rows]),
            "Wk_s": np.ascontiguousarray(Wk[rows]),
            "Wv_s": np.ascontiguousarray(Wv[rows]),
            "Wbeta_s": np.ascontiguousarray(Wbeta[4 * a:4 * (a + 1)]),
            "Wg1": Wg1,
            "Wg2_s": np.ascontiguousarray(Wg2[rows]),
            "gnorm_cols": np.ascontiguousarray(g_norm[rows].reshape(4, 128).T),
            "ident": ident,
            "mask_lowneg": mask_lowneg,
            "mask_triu": mask_triu,
        })
    res = run_bass_kernel_spmd(nc_attn, in_maps, core_ids=list(range(8)))
    og = [np.asarray(res.results[c]["og_out"]).view(ml_dtypes.bfloat16) for c in range(8)]
    og_b = [np.concatenate([og[4 * b + a] for a in range(4)], axis=0) for b in range(2)]

    ub_cols = np.ascontiguousarray(ub.reshape(64, 128).T)
    db_cols = np.ascontiguousarray(db.reshape(16, 128).T)
    in_maps = []
    for c in range(8):
        b, qd = divmod(c, 4)
        tsl = slice(1024 * qd, 1024 * (qd + 1))
        in_maps.append({
            "og_full": np.ascontiguousarray(og_b[b][:, tsl]),
            "x_slice": np.ascontiguousarray(x[b, tsl]),
            "Wo": Wo,
            "up_dq": up_dq,
            "dn_dq": dn_dq,
            "ub_cols": ub_cols,
            "db_cols": db_cols,
            "ident": ident,
        })
    res = run_bass_kernel_spmd(nc_tail, in_maps, core_ids=list(range(8)))
    out = np.empty((2, 4096, 2048), dtype=np.float32)
    for c in range(8):
        b, qd = divmod(c, 4)
        out[b, 1024 * qd:1024 * (qd + 1)] = np.asarray(res.results[c]["y_out"])
    return out


# version 02
# speedup vs baseline: 1.0003x; 1.0003x over previous
import numpy as np
from contextlib import ExitStack
import concourse.bass as bass
import concourse.mybir as mybir

F32 = mybir.dt.float32
BF16 = mybir.dt.bfloat16
AF = mybir.ActivationFunctionType
ALU = mybir.AluOpType
AX = mybir.AxisListType

_STAGES = [16] * 7 + [8, 4, 2, 1]


def _f32(x):
    return float(np.float32(x))


class SE:

    def __init__(self, eng, sem, self_chain=True):
        self.e = eng
        self.sem = sem
        self.cnt = 0
        self.self_chain = self_chain

    def w(self, sem_cnt):
        sem, val = sem_cnt
        if val > 0:
            self.e.wait_ge(sem, val)

    def mark(self):
        return (self.sem, self.cnt)

    def __getattr__(self, name):
        fn = getattr(self.e, name)
        if not callable(fn):
            return fn

        def wrapped(*a, **k):
            inst = fn(*a, **k)
            if self.self_chain and self.cnt > 0:
                inst._wait_ge(self.sem, self.cnt)
            self.cnt += 1
            inst.then_inc(self.sem, 1)
            return inst

        return wrapped


QUANT_VEC_OPS = 1 + 1 + 1 + 4 * len(_STAGES) + 1 + 1 + 4 + 2 + 1 + 2 + 1 + 1


def emit_quant_vec(V, qb, out_ap):
    aa, sg = qb["aa"], qb["sg"]
    amax, rr, ss, mm, tt = qb["amax"], qb["rr"], qb["ss"], qb["mm"], qb["tt"]
    xh, acc, bb, cc, qq = qb["xh"], qb["acc"], qb["bb"], qb["cc"], qb["qq"]
    NB = 16
    c0 = V.cnt
    V.tensor_reduce(out=amax[:], in_=aa[:].rearrange("p (a b) -> p a b", b=32),
                    axis=AX.X, op=ALU.max)
    V.tensor_scalar(out=mm[:], in0=amax[:], scalar1=_f32(1e-30),
                    scalar2=_f32(2.0 ** 100), op0=ALU.max, op1=ALU.mult)
    V.memset(rr[:], _f32(2.0 ** 102))
    for E in _STAGES:
        V.tensor_scalar(out=tt[:], in0=mm[:], scalar1=_f32(2.0 ** E),
                        scalar2=None, op0=ALU.is_lt)
        V.tensor_scalar(out=tt[:], in0=tt[:], scalar1=_f32(1.0 - 2.0 ** (-E)),
                        scalar2=_f32(2.0 ** (-E)), op0=ALU.mult, op1=ALU.add)
        V.tensor_tensor(out=mm[:], in0=mm[:], in1=tt[:], op=ALU.mult)
        V.tensor_tensor(out=rr[:], in0=rr[:], in1=tt[:], op=ALU.mult)
    V.reciprocal(out=ss[:], in_=rr[:])
    rbc = rr[:].unsqueeze(2).broadcast_to([128, NB, 32])
    V.tensor_tensor(out=xh[:].rearrange("p (a b) -> p a b", b=32),
                    in0=aa[:].rearrange("p (a b) -> p a b", b=32),
                    in1=rbc, op=ALU.mult)
    V.tensor_scalar(out=acc[:], in0=xh[:], scalar1=0.25, scalar2=None, op0=ALU.is_ge)
    for m in (0.75, 1.25, 1.75):
        V.scalar_tensor_tensor(out=acc[:], in0=xh[:], scalar=m, in1=acc[:],
                               op0=ALU.is_ge, op1=ALU.add)
    V.tensor_scalar(out=bb[:], in0=xh[:], scalar1=2.5, scalar2=None, op0=ALU.is_ge)
    V.scalar_tensor_tensor(out=bb[:], in0=xh[:], scalar=3.5, in1=bb[:],
                           op0=ALU.is_ge, op1=ALU.add)
    V.tensor_scalar(out=cc[:], in0=xh[:], scalar1=5.0, scalar2=None, op0=ALU.is_ge)
    V.scalar_tensor_tensor(out=qq[:], in0=acc[:], scalar=0.5, in1=bb[:],
                           op0=ALU.mult, op1=ALU.add)
    V.scalar_tensor_tensor(out=qq[:], in0=cc[:], scalar=2.0, in1=qq[:],
                           op0=ALU.mult, op1=ALU.add)
    sbc = ss[:].unsqueeze(2).broadcast_to([128, NB, 32])
    V.tensor_tensor(out=qq[:].rearrange("p (a b) -> p a b", b=32),
                    in0=qq[:].rearrange("p (a b) -> p a b", b=32),
                    in1=sbc, op=ALU.mult)
    V.tensor_tensor(out=out_ap, in0=qq[:], in1=sg[:], op=ALU.mult)
    assert V.cnt - c0 == QUANT_VEC_OPS, (V.cnt - c0, QUANT_VEC_OPS)


def quant_bufs(nc, tag=""):
    sb = nc.alloc_sbuf_tensor
    return {
        "aa": sb("q_aa" + tag, [128, 512], F32),
        "sg": sb("q_sg" + tag, [128, 512], F32),
        "amax": sb("q_amax" + tag, [128, 16], F32),
        "rr": sb("q_rr" + tag, [128, 16], F32),
        "ss": sb("q_ss" + tag, [128, 16], F32),
        "mm": sb("q_mm" + tag, [128, 16], F32),
        "tt": sb("q_tt" + tag, [128, 16], F32),
        "xh": sb("q_xh" + tag, [128, 512], F32),
        "acc": sb("q_acc" + tag, [128, 512], F32),
        "bb": sb("q_bb" + tag, [128, 512], F32),
        "cc": sb("q_cc" + tag, [128, 512], F32),
        "qq": sb("q_qq" + tag, [128, 512], F32),
    }



def build_dequant(shapes):
    nc = bass.Bass("TRN2", target_bir_lowering=False)
    ident_d = nc.dram_tensor("ident", [128, 128], F32, kind="ExternalInput").ap()
    ins, outs, strips = {}, {}, []
    for name, R, C in shapes:
        assert R % 128 == 0 and C % 512 == 0
        ins[name] = nc.dram_tensor(name, [R, C], F32, kind="ExternalInput").ap()
        outs[name] = nc.dram_tensor(name + "_dq", [R, C], BF16, kind="ExternalOutput").ap()
        for rt in range(R // 128):
            for cs in range(C // 512):
                strips.append((name, rt, cs))
    nst = len(strips)
    VC = QUANT_VEC_OPS + 1
    SC = 3
    TC = 8

    sb = nc.alloc_sbuf_tensor
    pst = nc.alloc_psum_tensor
    es = ExitStack()
    sem = lambda n: es.enter_context(nc.semaphore(n))

    ident_s = sb("ident_s", [128, 128], F32)
    identb = sb("identb", [128, 128], BF16)
    ldb = [sb("ld0", [128, 512], F32), sb("ld1", [128, 512], F32)]
    wt = sb("wt", [128, 512], F32)
    dqt = sb("dqt", [128, 512], BF16)
    stb = [sb("st0", [128, 512], BF16), sb("st1", [128, 512], BF16)]
    qb = quant_bufs(nc)
    psA = pst("psA", [128, 512], F32)
    psB = pst("psB", [128, 512], BF16)

    smV, smS, smT = sem("smV"), sem("smS"), sem("smT")
    idm = sem("idm")
    lds = [sem("lda"), sem("ldb")]
    sts = [sem("sta"), sem("stb")]

    def v_after_dqt(i):
        return 1 + i * VC + QUANT_VEC_OPS

    def v_after_staging(i):
        return 1 + (i + 1) * VC

    with nc.Block() as block:
        @block.sync
        def _(sync):
            sync.dma_start(out=ident_s[:], in_=ident_d[:]).then_inc(idm, 16)
            for i, (name, rt, cs) in enumerate(strips):
                inst = sync.dma_start(
                    out=ldb[i % 2][:],
                    in_=ins[name][rt * 128:(rt + 1) * 128, cs * 512:(cs + 1) * 512])
                if i >= 2:
                    inst._wait_ge(smT, TC * (i - 2) + 4)
                inst.then_inc(lds[i % 2], 16)

        @block.gpsimd
        def _(g):
            for i, (name, rt, cs) in enumerate(strips):
                g.wait_ge(smV, v_after_staging(i))
                g.dma_start(out=outs[name][rt * 128:(rt + 1) * 128, cs * 512:(cs + 1) * 512],
                            in_=stb[i % 2][:]).then_inc(sts[i % 2], 16)
            for par in range(2):
                n = len([1 for i in range(nst) if i % 2 == par])
                if n:
                    g.wait_ge(sts[par], 16 * n)

        @block.scalar
        def _(s):
            S = SE(s, smS)
            for i in range(nst):
                S.w((smT, TC * i + 4))
                if i > 0:
                    S.w((smV, v_after_dqt(i - 1)))
                S.copy(out=wt[:], in_=psA[:])
                S.activation(out=qb["aa"][:], in_=wt[:], func=AF.Abs)
                S.activation(out=qb["sg"][:], in_=wt[:], func=AF.Sign)
            assert S.cnt == SC * nst

        @block.vector
        def _(v):
            V = SE(v, smV)
            V.w((idm, 16))
            V.tensor_copy(out=identb[:], in_=ident_s[:])
            for i in range(nst):
                V.w((smS, SC * i + SC))
                emit_quant_vec(V, qb, dqt[:])
                V.w((smT, TC * i + TC))
                if i >= 2:
                    V.w((sts[i % 2], 16 * (i // 2)))
                V.tensor_copy(out=stb[i % 2][:], in_=psB[:])
            assert V.cnt == 1 + VC * nst

        @block.tensor
        def _(t):
            T = SE(t, smT, self_chain=False)
            T.w((idm, 16))
            T.w((smV, 1))
            for i in range(nst):
                T.w((lds[i % 2], 16 * (i // 2 + 1)))
                if i > 0:
                    T.w((smS, SC * (i - 1) + 1))
                buf = ldb[i % 2]
                for kk in range(4):
                    T.transpose(psA[:, kk * 128:(kk + 1) * 128],
                                buf[:, kk * 128:(kk + 1) * 128], ident_s[:])
                T.w((smV, v_after_dqt(i)))
                if i > 0:
                    T.w((smV, v_after_staging(i - 1)))
                for kk in range(4):
                    T.transpose(psB[:, kk * 128:(kk + 1) * 128],
                                dqt[:, kk * 128:(kk + 1) * 128], identb[:])
            assert T.cnt == TC * nst

    return nc, [n for n, _, _ in shapes]


def ref_dequant(w):
    FP4_GRID = np.array([0.0, 0.5, 1.0, 1.5, 2.0, 3.0, 4.0, 6.0], dtype=w.dtype)
    FP4_MIDS = np.array([0.25, 0.75, 1.25, 1.75, 2.5, 3.5, 5.0], dtype=w.dtype)
    K, N = w.shape
    wb = w.reshape(K // 32, 32, N)
    amax = np.max(np.abs(wb), axis=1, keepdims=True)
    expo = np.floor(np.log2(np.maximum(amax, 1e-30))) - 2.0
    scales = np.exp2(expo)
    x = wb / scales
    idx = np.digitize(np.abs(x), FP4_MIDS)
    q = np.sign(x) * FP4_GRID[idx]
    return (q * scales).reshape(K, N)




class PEng:
    def __init__(self, name, sems, eng=None, live=False, self_chain=True):
        self.name, self.sems, self.e = name, sems, eng
        self.live, self.self_chain, self.cnt = live, self_chain, 0

    def mark(self):
        return (self.name, self.cnt)

    def w(self, dep):
        if dep is None:
            return
        nm, val = dep
        if nm == self.name or val <= 0:
            return
        if self.live:
            self.e.wait_ge(self.sems[nm], val)

    def __getattr__(self, op):
        def f(*a, **k):
            self.cnt += 1
            if self.live:
                inst = getattr(self.e, op)(*a, **k)
                if self.self_chain and self.cnt > 1:
                    inst._wait_ge(self.sems[self.name], self.cnt - 1)
                inst.then_inc(self.sems[self.name], 1)
                return inst
            return None
        return f


class PDma:
    def __init__(self, name, sems, eng=None, live=False):
        self.name, self.sems, self.e = name, sems, eng
        self.live, self.cnt = live, 0

    def dma(self, out=None, in_=None, wait=None):
        self.cnt += 1
        if self.live:
            if self.cnt > 1:
                self.e.wait_ge(self.sems[self.name], 16 * (self.cnt - 1))
            inst = self.e.dma_start(out=out, in_=in_)
            if wait is not None:
                nm, val = wait
                if val > 0:
                    inst._wait_ge(self.sems[nm], val)
            inst.then_inc(self.sems[self.name], 16)
        return (self.name, self.cnt * 16)

    def mark(self):
        return (self.name, self.cnt * 16)

    def w(self, dep):
        if dep is None:
            return
        nm, val = dep
        if nm == self.name or val <= 0:
            return
        if self.live:
            self.e.wait_ge(self.sems[nm], val)

    def drain(self):
        if self.live and self.cnt:
            self.e.wait_ge(self.sems[self.name], self.cnt * 16)


def run_passes(nc, sems, prog):
    with nc.Block() as block:
        @block.sync
        def _(sync):
            prog(PEng("T", sems), PEng("V", sems), PEng("S", sems),
                 PDma("SY", sems, sync, True), PDma("GP", sems))

        @block.gpsimd
        def _(g):
            prog(PEng("T", sems), PEng("V", sems), PEng("S", sems),
                 PDma("SY", sems), PDma("GP", sems, g, True))

        @block.scalar
        def _(s):
            prog(PEng("T", sems), PEng("V", sems), PEng("S", sems, s, True),
                 PDma("SY", sems), PDma("GP", sems))

        @block.vector
        def _(v):
            prog(PEng("T", sems), PEng("V", sems, v, True), PEng("S", sems),
                 PDma("SY", sems), PDma("GP", sems))

        @block.tensor
        def _(t):
            prog(PEng("T", sems, t, True, self_chain=False), PEng("V", sems),
                 PEng("S", sems), PDma("SY", sems), PDma("GP", sems))



def build_attn(n_tok, dbg_at=None):
    assert n_tok % 512 == 0
    NS = n_tok // 512
    nc = bass.Bass("TRN2", target_bir_lowering=False)
    dr = lambda n, s, dt=F32, k="ExternalInput": nc.dram_tensor(n, s, dt, kind=k).ap()
    xb = dr("xb", [n_tok, 2048])
    Wq_s = dr("Wq_s", [512, 2048])
    Wk_s = dr("Wk_s", [512, 2048])
    Wv_s = dr("Wv_s", [512, 2048])
    Wbeta_s = dr("Wbeta_s", [4, 2048])
    Wg1 = dr("Wg1", [128, 2048])
    Wg2_s = dr("Wg2_s", [512, 128])
    gnorm_cols = dr("gnorm_cols", [128, 4])
    ident_d = dr("ident", [128, 128])
    mask_ln_d = dr("mask_lowneg", [128, 128])
    mask_tu_d = dr("mask_triu", [128, 128], BF16)
    og_out = dr("og_out", [512, n_tok], F32, k="ExternalOutput")
    dbg_out = dr("dbg_out", [128, 512], F32, k="ExternalOutput") if dbg_at is not None else None

    sb = nc.alloc_sbuf_tensor
    pst = nc.alloc_psum_tensor
    es = ExitStack()

    ident_s = sb("ident_s", [128, 128], F32)
    identb = sb("identb", [128, 128], BF16)
    mask_ln = sb("mask_ln", [128, 128], F32)
    mask_tu = sb("mask_tu", [128, 128], BF16)
    gnc = sb("gnc", [128, 4], F32)
    wqT = sb("wqT", [128, 16 * 512], BF16)
    wkT = sb("wkT", [128, 16 * 512], BF16)
    wvT = sb("wvT", [128, 16 * 512], BF16)
    wbT = sb("wbT", [128, 16 * 4], BF16)
    wg1T = sb("wg1T", [128, 16 * 128], BF16)
    wg2T = sb("wg2T", [128, 4 * 128], BF16)
    wld = [sb("wld0", [128, 2048], F32), sb("wld1", [128, 2048], F32)]
    x_tm = [sb("x_tm0", [128, 2048], F32), sb("x_tm1", [128, 2048], F32)]
    x_cm = sb("x_cm", [128, 16 * 512], BF16)
    q_h = [sb(f"q_{h}", [128, 512], BF16) for h in range(4)]
    kr_h = [sb(f"kr_{h}", [128, 512], BF16) for h in range(4)]
    k_h = [sb(f"k_{h}", [128, 512], BF16) for h in range(4)]
    v_h = [sb(f"v_{h}", [128, 512], BF16) for h in range(4)]
    ksq = sb("ksq", [128, 512], F32)
    xg1 = sb("xg1", [128, 512], BF16)
    beta_sb = sb("beta_sb", [4, 512], F32)
    rowb = [sb(f"rowb{i}", [1, 512], F32) for i in range(3)]
    ones_col = sb("ones_col", [128, 1], F32)
    ones_row = sb("ones_row", [1, 128], F32)
    Ablk = sb("Ablk", [32, 10 * 32], F32)
    Ablk_bf = sb("Ablk_bf", [32, 6 * 32], BF16)
    W32 = [sb(f"W32_{j}", [32, 32], F32) for j in range(5)]
    Z32 = [sb(f"Z32_{j}", [32, 32], F32) for j in range(5)]
    Gg = [sb("Gg_a", [32, 32], F32), sb("Gg_c", [32, 32], F32)]
    PTb4 = sb("PTb4", [32, 4 * 128], BF16)
    Rpre = sb("Rpre", [128, 128], BF16)
    rb4 = sb("rb4", [32, 4 * 128], F32)
    Ub4 = sb("Ub4", [32, 4 * 128], BF16)
    Ktb4 = sb("Ktb4", [32, 4 * 128], BF16)
    betaT4 = sb("betaT4", [32, 16 * 4], F32)
    nbetaT4 = sb("nbetaT4", [32, 16 * 4], F32)
    S_f = [sb(f"Sf_{h}", [128, 128], F32) for h in range(4)]
    S_b = [sb(f"Sb_{h}", [128, 128], BF16) for h in range(4)]
    ogf = sb("ogf", [128, 128], F32)
    ogsq = sb("ogsq", [128, 128], F32)
    gate = sb("gate", [128, 128], BF16)
    og_st = [sb(f"og_st{h}", [128, 512], F32) for h in range(4)]
    rb_s = sb("rb_s", [1, 128], F32)
    rb_t = sb("rb_t", [1, 128], F32)
    rb_u = sb("rb_u", [1, 128], F32)
    rb_v = sb("rb_v", [1, 128], F32)

    psX = pst("psX", [128, 512], F32)
    psQ = pst("psQ", [128, 512], F32)
    psC = pst("psC", [128, 512], F32)
    psR = pst("psR", [128, 512], F32)
    psBt = pst("psBt", [128, 1024], BF16)
    psRow = pst("psRow", [128, 512], F32)

    sems = {}
    for nm in ("T", "V", "S", "SY", "GP"):
        sems[nm] = es.enter_context(nc.semaphore("sem" + nm))

    EPS = _f32(1e-6)

    def prog(T, V, S, SY, GP):
        SY.dma(ident_s[:], ident_d)
        SY.dma(mask_ln[:], mask_ln_d)
        SY.dma(mask_tu[:], mask_tu_d)
        m_cst = SY.dma(gnc[:], gnorm_cols)
        V.w(m_cst)
        V.tensor_copy(out=identb[:], in_=ident_s[:])
        V.memset(ones_col[:], 1.0)
        V.memset(ones_row[:], 1.0)
        for h in range(4):
            V.memset(S_f[h][:], 0.0)
            V.memset(S_b[h][:], 0.0)

        loads = []
        for mat in range(3):
            for m in range(4):
                loads.append(("qkv", mat, m))
        loads.append(("wg1", 0, 0))
        loads.append(("wbeta", 0, 0))
        for t4 in range(4):
            loads.append(("wg2", 0, t4))
        tmark_after_load = [None] * len(loads)
        vmark_after_load = [None] * len(loads)
        ld_marks = []
        for i, (kind, mat, m) in enumerate(loads):
            src = None
            if kind == "qkv":
                src = (Wq_s, Wk_s, Wv_s)[mat][m * 128:(m + 1) * 128, :]
                dst = wld[i % 2][:, :]
            elif kind == "wg1":
                src = Wg1
                dst = wld[i % 2][:, :]
            elif kind == "wbeta":
                src = Wbeta_s
                dst = wld[i % 2][0:4, :]
            else:
                src = Wg2_s[m * 128:(m + 1) * 128, :]
                dst = wld[i % 2][:, 0:128]
            wait = tmark_after_load[i - 2] if i >= 2 else None
            ld_marks.append(SY.dma(dst, src, wait=wait))

            T.w(ld_marks[i])
            T.w(V.mark())
            if kind == "qkv":
                dstT = (wqT, wkT, wvT)[mat]
                for g in range(4):
                    if g > 0:
                        T.w(V.mark())
                    for kk4 in range(4):
                        kk = g * 4 + kk4
                        T.transpose(psX[:, kk4 * 128:(kk4 + 1) * 128],
                                    wld[i % 2][:, kk * 128:(kk + 1) * 128], ident_s[:])
                    mt = T.mark()
                    V.w(mt)
                    for kk4 in range(4):
                        kk = g * 4 + kk4
                        V.tensor_copy(out=dstT[:, kk * 512 + m * 128: kk * 512 + m * 128 + 128],
                                      in_=psX[:, kk4 * 128:(kk4 + 1) * 128])
            elif kind == "wg1":
                for g in range(4):
                    if g > 0:
                        T.w(V.mark())
                    for kk4 in range(4):
                        kk = g * 4 + kk4
                        T.transpose(psX[:, kk4 * 128:(kk4 + 1) * 128],
                                    wld[i % 2][:, kk * 128:(kk + 1) * 128], ident_s[:])
                    mt = T.mark()
                    V.w(mt)
                    for kk4 in range(4):
                        kk = g * 4 + kk4
                        V.tensor_copy(out=wg1T[:, kk * 128:(kk + 1) * 128],
                                      in_=psX[:, kk4 * 128:(kk4 + 1) * 128])
            elif kind == "wbeta":
                for g in range(4):
                    if g > 0:
                        T.w(V.mark())
                    for kk4 in range(4):
                        kk = g * 4 + kk4
                        T.transpose(psX[:, kk4 * 128:kk4 * 128 + 4],
                                    wld[i % 2][0:4, kk * 128:(kk + 1) * 128], ident_s[0:4, 0:4])
                    mt = T.mark()
                    V.w(mt)
                    for kk4 in range(4):
                        kk = g * 4 + kk4
                        V.tensor_copy(out=wbT[:, kk * 4:(kk + 1) * 4],
                                      in_=psX[:, kk4 * 128:kk4 * 128 + 4])
            else:
                T.transpose(psX[:, 0:128], wld[i % 2][:, 0:128], ident_s[:])
                mt = T.mark()
                V.w(mt)
                V.tensor_copy(out=wg2T[:, m * 128:(m + 1) * 128], in_=psX[:, 0:128])
            tmark_after_load[i] = T.mark()
            vmark_after_load[i] = V.mark()

        xld_marks = {}
        t_after_x = {}
        for si in range(NS):
            for tt in range(4):
                gi = si * 4 + tt
                wait = t_after_x.get(gi - 2)
                xld_marks[gi] = SY.dma(x_tm[gi % 2][:],
                                       xb[gi * 128:(gi + 1) * 128, :], wait=wait)
                T.w(xld_marks[gi])
                T.w(V.mark())
                for g in range(4):
                    if g > 0:
                        T.w(V.mark())
                    for kk4 in range(4):
                        kk = g * 4 + kk4
                        T.transpose(psX[:, kk4 * 128:(kk4 + 1) * 128],
                                    x_tm[gi % 2][:, kk * 128:(kk + 1) * 128], ident_s[:])
                    mt = T.mark()
                    V.w(mt)
                    for kk4 in range(4):
                        kk = g * 4 + kk4
                        V.tensor_copy(out=x_cm[:, kk * 512 + tt * 128: kk * 512 + tt * 128 + 128],
                                      in_=psX[:, kk4 * 128:(kk4 + 1) * 128])
                t_after_x[gi] = T.mark()

            m_xcm = V.mark()

            T.w(m_xcm)
            for kk in range(16):
                T.matmul(psRow[0:4, :], wbT[:, kk * 4:(kk + 1) * 4],
                         x_cm[:, kk * 512:(kk + 1) * 512],
                         start=(kk == 0), stop=(kk == 15))
            S.w(T.mark())
            S.activation(out=beta_sb[:], in_=psRow[0:4, :], func=AF.Sigmoid)
            V.w(S.mark())
            V.tensor_scalar(out=beta_sb[:], in0=beta_sb[:], scalar1=2.0,
                            scalar2=None, op0=ALU.mult)
            T.w(V.mark())
            for j16 in range(16):
                T.transpose(psX[0:32, j16 * 4:j16 * 4 + 4],
                            beta_sb[0:4, j16 * 32:(j16 + 1) * 32], ident_s[0:4, 0:4])
            V.w(T.mark())
            V.tensor_copy(out=betaT4[:, :], in_=psX[0:32, 0:64])
            V.tensor_scalar(out=nbetaT4[:, :], in0=betaT4[:, :], scalar1=-1.0,
                            scalar2=None, op0=ALU.mult)

            for h in range(4):
                for mat, (wT, dst) in enumerate(((wqT, q_h[h]), (wkT, kr_h[h]), (wvT, v_h[h]))):
                    T.w(V.mark())
                    T.w(S.mark())
                    for kk in range(16):
                        T.matmul(psQ[:, :], wT[:, kk * 512 + h * 128: kk * 512 + h * 128 + 128],
                                 x_cm[:, kk * 512:(kk + 1) * 512],
                                 start=(kk == 0), stop=(kk == 15))
                    mt = T.mark()
                    S.w(mt)
                    if mat == 0:
                        S.activation(out=dst[:], in_=psQ[:, :], func=AF.Silu)
                    elif mat == 1:
                        S.activation(out=dst[:], in_=psQ[:, :], func=AF.Silu)
                    else:
                        S.copy(out=dst[:], in_=psQ[:, :])
                V.w(S.mark())
                V.tensor_tensor(out=ksq[:], in0=kr_h[h][:], in1=kr_h[h][:], op=ALU.mult)
                T.w(V.mark())
                T.matmul(psRow[0:1, :], ones_col[:], ksq[:], start=True, stop=True)
                V.w(T.mark())
                V.tensor_scalar(out=rowb[0][:], in0=psRow[0:1, :], scalar1=EPS,
                                scalar2=None, op0=ALU.add)
                S.w(V.mark())
                S.activation(out=rowb[1][:], in_=rowb[0][:], func=AF.Sqrt)
                V.w(S.mark())
                V.reciprocal(out=rowb[2][:], in_=rowb[1][:])
                V.tensor_tensor(out=rowb[1][:], in0=rowb[2][:], in1=rowb[2][:], op=ALU.mult)
                V.tensor_tensor(out=rowb[1][:], in0=rowb[1][:], in1=rowb[0][:], op=ALU.mult)
                V.tensor_scalar(out=rowb[1][:], in0=rowb[1][:], scalar1=-0.5,
                                scalar2=1.5, op0=ALU.mult, op1=ALU.add)
                V.tensor_tensor(out=rowb[2][:], in0=rowb[2][:], in1=rowb[1][:], op=ALU.mult)
                T.w(V.mark())
                T.matmul(psX[:, 0:512], ones_row[:], rowb[2][:], start=True, stop=True)
                V.w(T.mark())
                V.tensor_tensor(out=k_h[h][:], in0=kr_h[h][:], in1=psX[:, :], op=ALU.mult)

            T.w(V.mark())
            for kk in range(16):
                T.matmul(psQ[:, :], wg1T[:, kk * 128:(kk + 1) * 128],
                         x_cm[:, kk * 512:(kk + 1) * 512],
                         start=(kk == 0), stop=(kk == 15))
            V.w(T.mark())
            V.tensor_copy(out=xg1[:], in_=psQ[:, :])

            for c_loc in range(4):
                for h in range(4):
                    cs = slice(c_loc * 128, (c_loc + 1) * 128)
                    Kc = k_h[h][:, cs]
                    Qc = q_h[h][:, cs]
                    Vc = v_h[h][:, cs]

                    def sb32(b):
                        return slice((c_loc * 4 + b) * 32, (c_loc * 4 + b) * 32 + 32)

                    def bcol(b, neg=False):
                        t = nbetaT4 if neg else betaT4
                        j = (c_loc * 4 + b) * 4 + h
                        return t[:, j:j + 1]

                    pairs = [(b, b) for b in range(4)] + \
                            [(b, b2) for b in range(4) for b2 in range(b + 1, 4)]
                    T.w(V.mark())
                    for idx, (b, b2) in enumerate(pairs):
                        T.matmul(psQ[0:32, idx * 32:(idx + 1) * 32],
                                 Kc[:, b * 32:(b + 1) * 32], Kc[:, b2 * 32:(b2 + 1) * 32],
                                 start=True, stop=True)
                    V.w(T.mark())
                    V.tensor_copy(out=Ablk[:, :], in_=psQ[0:32, 0:320])
                    V.tensor_copy(out=Ablk_bf[:, :], in_=psQ[0:32, 128:320])
                    T.w(V.mark())
                    T.matmul(psR[:, 0:128], S_b[h][:], Kc, start=True, stop=True)
                    V.w(T.mark())
                    V.scalar_tensor_tensor(out=Rpre[:], in0=psR[:, 0:128], scalar=-1.0,
                                           in1=Vc, op0=ALU.mult, op1=ALU.add)
                    T.w(V.mark())
                    for b in range(4):
                        T.transpose(psBt[0:32, b * 128:(b + 1) * 128],
                                    Rpre[:, b * 32:(b + 1) * 32], identb[:])
                        T.transpose(psBt[0:32, 512 + b * 128:512 + (b + 1) * 128],
                                    Kc[:, b * 32:(b + 1) * 32], identb[:])
                    V.w(T.mark())
                    for b in range(4):
                        V.tensor_scalar(out=rb4[:, b * 128:(b + 1) * 128],
                                        in0=psBt[0:32, b * 128:(b + 1) * 128],
                                        scalar1=bcol(b), scalar2=None, op0=ALU.mult)
                        V.tensor_copy(out=Ktb4[:, b * 128:(b + 1) * 128],
                                      in_=psBt[0:32, 512 + b * 128:512 + (b + 1) * 128])
                    T.w(V.mark())
                    for b in range(4):
                        T.matmul(psQ[0:32, 320 + 0:320 + 128] if False else
                                 psC[0:32, b * 128:(b + 1) * 128],
                                 Kc[:, b * 32:(b + 1) * 32], Qc, start=True, stop=True)
                    V.w(T.mark())
                    for b in range(4):
                        V.tensor_tensor(out=PTb4[:, b * 128:(b + 1) * 128],
                                        in0=psC[0:32, b * 128:(b + 1) * 128],
                                        in1=mask_tu[b * 32:b * 32 + 32, :], op=ALU.mult)
                    if dbg_at == ("PT", si, c_loc, h):
                        V.tensor_copy(out=x_tm[0][0:32, 0:512], in_=PTb4[:, 0:512])
                        SY.dma(dbg_out[0:32, :], x_tm[0][0:32, 0:512], wait=V.mark())
                    for b in range(4):
                        didx = b
                        V.scalar_tensor_tensor(out=W32[0][:], in0=Ablk[:, didx * 32:(didx + 1) * 32],
                                               scalar=bcol(b), in1=mask_ln[0:32, 0:32],
                                               op0=ALU.mult, op1=ALU.mult)
                        T.w(V.mark())
                        T.transpose(psC[0:32, 448:480], W32[0][:], ident_s[0:32, 0:32])
                        V.w(T.mark())
                        V.tensor_copy(out=Z32[0][:], in_=psC[0:32, 448:480])
                        for j in range(1, 5):
                            T.w(V.mark())
                            T.matmul(psC[0:32, 448:480], Z32[j - 1][:], W32[j - 1][:], start=True, stop=True)
                            T.matmul(psC[0:32, 480:512], W32[j - 1][:], Z32[j - 1][:], start=True, stop=True)
                            V.w(T.mark())
                            V.tensor_copy(out=W32[j][:], in_=psC[0:32, 448:480])
                            V.tensor_copy(out=Z32[j][:], in_=psC[0:32, 480:512])
                        V.tensor_tensor(out=Gg[0][:], in0=Z32[0][:], in1=ident_s[0:32, 0:32], op=ALU.add)
                        for j in range(1, 5):
                            T.w(V.mark())
                            T.matmul(psC[0:32, 448:480], W32[j][:], Gg[(j - 1) % 2][:], start=True, stop=True)
                            V.w(T.mark())
                            V.tensor_tensor(out=Gg[j % 2][:], in0=psC[0:32, 448:480],
                                            in1=Gg[(j - 1) % 2][:], op=ALU.add)
                        G4 = Gg[0]
                        T.w(V.mark())
                        T.matmul(psR[0:32, 128:256], G4[:], rb4[:, b * 128:(b + 1) * 128],
                                 start=True, stop=True)
                        V.w(T.mark())
                        V.tensor_copy(out=Ub4[:, b * 128:(b + 1) * 128], in_=psR[0:32, 128:256])
                        for b2 in range(b + 1, 4):
                            pidx = 4 + [(bb, bb2) for bb in range(4) for bb2 in range(bb + 1, 4)].index((b, b2))
                            T.w(V.mark())
                            T.matmul(psR[0:32, 256:384],
                                     Ablk_bf[:, (pidx - 4) * 32:(pidx - 3) * 32],
                                     Ub4[:, b * 128:(b + 1) * 128], start=True, stop=True)
                            V.w(T.mark())
                            V.scalar_tensor_tensor(out=rb4[:, b2 * 128:(b2 + 1) * 128],
                                                   in0=psR[0:32, 256:384], scalar=bcol(b2, neg=True),
                                                   in1=rb4[:, b2 * 128:(b2 + 1) * 128],
                                                   op0=ALU.mult, op1=ALU.add)
                    if dbg_at == ("PT2", si, c_loc, h):
                        V.tensor_copy(out=x_tm[0][0:32, 0:512], in_=PTb4[:, 0:512])
                        SY.dma(dbg_out[0:32, :], x_tm[0][0:32, 0:512], wait=V.mark())
                    T.w(V.mark())
                    T.matmul(psR[:, 384:512], S_b[h][:], Qc, start=True, stop=False)
                    for b in range(4):
                        T.matmul(psR[:, 384:512], Ub4[:, b * 128:(b + 1) * 128],
                                 PTb4[:, b * 128:(b + 1) * 128],
                                 start=False, stop=(b == 3))
                    if dbg_at == (si, c_loc, h):
                        V.w(T.mark())
                        V.tensor_copy(out=x_tm[0][:, 0:128], in_=psR[:, 0:128])
                        V.tensor_copy(out=x_tm[0][:, 384:512], in_=psR[:, 384:512])
                        mdbg = V.mark()
                        SY.dma(dbg_out, x_tm[0][:, 0:512], wait=mdbg)
                    for b in range(4):
                        T.matmul(psQ[:, 384:512], Ktb4[:, b * 128:(b + 1) * 128],
                                 Ub4[:, b * 128:(b + 1) * 128],
                                 start=(b == 0), stop=(b == 3))
                    V.w(T.mark())
                    V.tensor_tensor(out=S_f[h][:], in0=S_f[h][:], in1=psQ[:, 384:512], op=ALU.add)
                    V.tensor_copy(out=S_b[h][:], in_=S_f[h][:])
                    T.matmul(psRow[:, 0:128], wg2T[:, h * 128:(h + 1) * 128],
                             xg1[:, cs], start=True, stop=True)
                    S.w(T.mark())
                    S.activation(out=gate[:], in_=psRow[:, 0:128], func=AF.Sigmoid)
                    V.w(S.mark())
                    V.tensor_tensor(out=ogf[:], in0=psR[:, 384:512], in1=gate[:], op=ALU.mult)
                    V.tensor_tensor(out=ogsq[:], in0=ogf[:], in1=ogf[:], op=ALU.mult)
                    T.w(V.mark())
                    T.matmul(psRow[0:1, 128:256], ones_col[:], ogsq[:], start=True, stop=True)
                    V.w(T.mark())
                    V.tensor_scalar(out=rb_s[:], in0=psRow[0:1, 128:256],
                                    scalar1=_f32(1.0 / 128.0), scalar2=EPS,
                                    op0=ALU.mult, op1=ALU.add)
                    S.w(V.mark())
                    S.activation(out=rb_t[:], in_=rb_s[:], func=AF.Sqrt)
                    V.w(S.mark())
                    V.reciprocal(out=rb_u[:], in_=rb_t[:])
                    V.tensor_tensor(out=rb_v[:], in0=rb_u[:], in1=rb_u[:], op=ALU.mult)
                    V.tensor_tensor(out=rb_v[:], in0=rb_v[:], in1=rb_s[:], op=ALU.mult)
                    V.tensor_scalar(out=rb_v[:], in0=rb_v[:], scalar1=-0.5,
                                    scalar2=1.5, op0=ALU.mult, op1=ALU.add)
                    V.tensor_tensor(out=rb_u[:], in0=rb_u[:], in1=rb_v[:], op=ALU.mult)
                    T.w(V.mark())
                    T.matmul(psRow[:, 256:384], ones_row[:], rb_u[0:1, :], start=True, stop=True)
                    V.w(T.mark())
                    V.tensor_tensor(out=ogf[:], in0=ogf[:], in1=psRow[:, 256:384], op=ALU.mult)
                    if si > 0 and c_loc == 0 and h == 0:
                        V.w(("GP", 16 * 4 * si))
                    V.tensor_scalar(out=og_st[h][:, cs], in0=ogf[:],
                                    scalar1=gnc[:, h:h + 1], scalar2=None, op0=ALU.mult)

            mv = V.mark()
            for h in range(4):
                GP.dma(og_out[h * 128:(h + 1) * 128, si * 512:(si + 1) * 512],
                       og_st[h][:], wait=mv)
        GP.drain()
        SY.drain()

    run_passes(nc, sems, prog)
    return nc



def build_tail(n_tok):
    assert n_tok % 512 == 0
    NH = n_tok // 512
    nc = bass.Bass("TRN2", target_bir_lowering=False)
    dr = lambda n, s, dt=F32, k="ExternalInput": nc.dram_tensor(n, s, dt, kind=k).ap()
    og_d = dr("og_full", [2048, n_tok])
    x_d = dr("x_slice", [n_tok, 2048])
    Wo_d = dr("Wo", [2048, 2048])
    up_d = dr("up_dq", [2048, 8192], BF16)
    dn_d = dr("dn_dq", [8192, 2048], BF16)
    ub_d = dr("ub_cols", [128, 64])
    db_d = dr("db_cols", [128, 16])
    ident_d = dr("ident", [128, 128])
    y_d = dr("y_out", [n_tok, 2048], F32, k="ExternalOutput")

    sb = nc.alloc_sbuf_tensor
    pst = nc.alloc_psum_tensor
    es = ExitStack()

    ident_s = sb("ident_s", [128, 128], F32)
    ub = sb("ub", [128, 64], F32)
    db = sb("db", [128, 16], F32)
    _obase = nc.bump_sbuf(65536)[0]
    h_sb = nc.alloc_sbuf_tensor_at("h_sb", [128, 64 * 512], BF16, offset=_obase)
    x_cm = nc.alloc_sbuf_tensor_at("x_cm", [128, 16 * 512], F32, offset=_obase)
    og_sb = nc.alloc_sbuf_tensor_at("og_sb", [128, 16 * 512], BF16, offset=_obase + 32768)
    x_tm = [nc.alloc_sbuf_tensor_at("x_tm0", [128, 2048], F32, offset=_obase + 49152),
            nc.alloc_sbuf_tensor_at("x_tm1", [128, 2048], F32, offset=_obase + 57344)]
    wld = x_tm
    woT = sb("woT", [128, 16 * 128], BF16)
    y_f = sb("y_f", [128, 16 * 512], F32)
    _xbase = nc.bump_sbuf(16384)[0]
    x1b = nc.alloc_sbuf_tensor_at("x1b", [128, 16 * 512], BF16, offset=_xbase)
    outb = nc.alloc_sbuf_tensor_at("outb", [128, 2048], F32, offset=_xbase + 8192)
    sqb = sb("sqb", [128, 512], F32)
    upw = [sb("upw0", [128, 16 * 256], BF16), sb("upw1", [128, 16 * 256], BF16)]
    dnw = [sb("dnw0", [128, 64 * 128], BF16), sb("dnw1", [128, 64 * 128], BF16)]
    z_f = sb("z_f", [128, 16 * 512], F32)
    rowm = sb("rowm", [1, 512], F32)
    rowq = sb("rowq", [1, 512], F32)
    rowr = sb("rowr", [1, 512], F32)
    rowt = sb("rowt", [1, 512], F32)
    ones_col = sb("ones_col", [128, 1], F32)
    ones_row = sb("ones_row", [1, 128], F32)

    psA = pst("psA", [128, 512], F32)
    psB_ = pst("psB", [128, 512], F32)
    psT = pst("psT", [128, 512], F32)
    psRow = pst("psRow", [128, 512], F32)

    sems = {}
    for nm in ("T", "V", "S", "SY", "GP"):
        sems[nm] = es.enter_context(nc.semaphore("sem" + nm))

    def ln_rows(T, V, S, src_tiles, eps):
        T.w(V.mark())
        for k in range(16):
            T.matmul(psRow[0:1, 0:512], ones_col[:], src_tiles[k],
                     start=(k == 0), stop=(k == 15))
        V.w(T.mark())
        V.tensor_scalar(out=rowm[:], in0=psRow[0:1, 0:512], scalar1=_f32(1.0 / 2048.0),
                        scalar2=None, op0=ALU.mult)
        for k in range(16):
            V.tensor_tensor(out=sqb[:], in0=src_tiles[k], in1=src_tiles[k], op=ALU.mult)
            T.w(V.mark())
            T.matmul(psRow[0:1, 0:512], ones_col[:], sqb[:],
                     start=(k == 0), stop=(k == 15))
            if k < 15:
                V.w(T.mark())
        V.w(T.mark())
        V.tensor_scalar(out=rowq[:], in0=psRow[0:1, 0:512], scalar1=_f32(1.0 / 2048.0),
                        scalar2=None, op0=ALU.mult)
        V.tensor_tensor(out=rowt[:], in0=rowm[:], in1=rowm[:], op=ALU.mult)
        V.tensor_tensor(out=rowq[:], in0=rowq[:], in1=rowt[:], op=ALU.subtract)
        V.tensor_scalar(out=rowq[:], in0=rowq[:], scalar1=_f32(eps), scalar2=None, op0=ALU.add)
        S.w(V.mark())
        S.activation(out=rowt[:], in_=rowq[:], func=AF.Sqrt)
        V.w(S.mark())
        V.reciprocal(out=rowr[:], in_=rowt[:])
        V.tensor_tensor(out=rowt[:], in0=rowr[:], in1=rowr[:], op=ALU.mult)
        V.tensor_tensor(out=rowt[:], in0=rowt[:], in1=rowq[:], op=ALU.mult)
        V.tensor_scalar(out=rowt[:], in0=rowt[:], scalar1=-0.5, scalar2=1.5,
                        op0=ALU.mult, op1=ALU.add)
        V.tensor_tensor(out=rowr[:], in0=rowr[:], in1=rowt[:], op=ALU.mult)
        T.w(V.mark())
        T.matmul(psRow[:, 0:512], ones_row[:], rowm[0:1, 0:128], start=True, stop=False)
        T.matmul(psRow[:, 0:512], ones_row[:], rowm[0:1, 0:128], start=False, stop=True)

    EPS_LN = 1e-5

    def prog(T, V, S, SY, GP):
        m_c = SY.dma(ident_s[:], ident_d)
        SY.dma(ub[:], ub_d)
        m_c2 = SY.dma(db[:], db_d)
        V.w(m_c2)
        V.memset(ones_col[:], 1.0)
        V.memset(ones_row[:], 1.0)

        for half in range(NH):
            t0 = half * 512
            for tt in range(4):
                mld = SY.dma(x_tm[tt % 2][:], x_d[t0 + tt * 128: t0 + (tt + 1) * 128, :],
                             wait=T.mark())
                T.w(mld)
                T.w(V.mark())
                for g in range(4):
                    if g > 0:
                        T.w(V.mark())
                    for kk4 in range(4):
                        kk = g * 4 + kk4
                        T.transpose(psT[:, kk4 * 128:(kk4 + 1) * 128],
                                    x_tm[tt % 2][:, kk * 128:(kk + 1) * 128], ident_s[:])
                    V.w(T.mark())
                    for kk4 in range(4):
                        kk = g * 4 + kk4
                        V.tensor_copy(out=x_cm[:, kk * 512 + tt * 128: kk * 512 + tt * 128 + 128],
                                      in_=psT[:, kk4 * 128:(kk4 + 1) * 128])
            m_og = GP.dma(og_sb[:].rearrange("p (a c) -> p a c", a=16),
                          og_d[:, t0:t0 + 512].rearrange("(a p) c -> p a c", p=128),
                          wait=T.mark())
            for m in range(16):
                mld = SY.dma(wld[m % 2][:], Wo_d[m * 128:(m + 1) * 128, :],
                             wait=T.mark())
                T.w(mld)
                T.w(V.mark())
                for g in range(4):
                    if g > 0:
                        T.w(V.mark())
                    for kk4 in range(4):
                        kk = g * 4 + kk4
                        T.transpose(psT[:, kk4 * 128:(kk4 + 1) * 128],
                                    wld[m % 2][:, kk * 128:(kk + 1) * 128], ident_s[:])
                    V.w(T.mark())
                    for kk4 in range(4):
                        kk = g * 4 + kk4
                        V.tensor_copy(out=woT[:, kk * 128:(kk + 1) * 128],
                                      in_=psT[:, kk4 * 128:(kk4 + 1) * 128])
                T.w(V.mark())
                T.w(m_og)
                for k in range(16):
                    T.matmul(psA[:, 0:512], woT[:, k * 128:(k + 1) * 128],
                             og_sb[:, k * 512:(k + 1) * 512],
                             start=(k == 0), stop=(k == 15))
                V.w(T.mark())
                V.tensor_tensor(out=y_f[:, m * 512:(m + 1) * 512], in0=psA[:, 0:512],
                                in1=x_cm[:, m * 512:(m + 1) * 512], op=ALU.add)
            ytiles = [y_f[:, k * 512:(k + 1) * 512] for k in range(16)]
            T.w(V.mark())
            for k in range(16):
                T.matmul(psRow[0:1, 0:512], ones_col[:], ytiles[k],
                         start=(k == 0), stop=(k == 15))
            V.w(T.mark())
            V.tensor_scalar(out=rowm[:], in0=psRow[0:1, 0:512],
                            scalar1=_f32(1.0 / 2048.0), scalar2=None, op0=ALU.mult)
            for k in range(16):
                V.tensor_tensor(out=sqb[:], in0=ytiles[k], in1=ytiles[k], op=ALU.mult)
                T.w(V.mark())
                T.matmul(psRow[0:1, 0:512], ones_col[:], sqb[:],
                         start=(k == 0), stop=(k == 15))
                V.w(T.mark())
            V.tensor_scalar(out=rowq[:], in0=psRow[0:1, 0:512],
                            scalar1=_f32(1.0 / 2048.0), scalar2=None, op0=ALU.mult)
            V.tensor_tensor(out=rowt[:], in0=rowm[:], in1=rowm[:], op=ALU.mult)
            V.tensor_tensor(out=rowq[:], in0=rowq[:], in1=rowt[:], op=ALU.subtract)
            V.tensor_scalar(out=rowq[:], in0=rowq[:], scalar1=_f32(EPS_LN),
                            scalar2=None, op0=ALU.add)
            S.w(V.mark())
            S.activation(out=rowt[:], in_=rowq[:], func=AF.Sqrt)
            V.w(S.mark())
            V.reciprocal(out=rowr[:], in_=rowt[:])
            V.tensor_tensor(out=rowt[:], in0=rowr[:], in1=rowr[:], op=ALU.mult)
            V.tensor_tensor(out=rowt[:], in0=rowt[:], in1=rowq[:], op=ALU.mult)
            V.tensor_scalar(out=rowt[:], in0=rowt[:], scalar1=-0.5, scalar2=1.5,
                            op0=ALU.mult, op1=ALU.add)
            V.tensor_tensor(out=rowr[:], in0=rowr[:], in1=rowt[:], op=ALU.mult)
            T.w(V.mark())
            T.matmul(psA[:, 0:512], ones_row[:], rowm[0:1, 0:512], start=True, stop=True)
            T.matmul(psB_[:, 0:512], ones_row[:], rowr[0:1, 0:512], start=True, stop=True)
            V.w(T.mark())
            for k in range(16):
                V.tensor_tensor(out=y_f[:, k * 512:(k + 1) * 512], in0=ytiles[k],
                                in1=psA[:, 0:512], op=ALU.subtract)
                V.tensor_tensor(out=y_f[:, k * 512:(k + 1) * 512], in0=ytiles[k],
                                in1=psB_[:, 0:512], op=ALU.mult)
                V.tensor_copy(out=x1b[:, k * 512:(k + 1) * 512], in_=ytiles[k])
            for mg in range(32):
                src = up_d[:, mg * 256:(mg + 1) * 256].rearrange("(a p) c -> p a c", p=128)
                mld = SY.dma(upw[mg % 2][:].rearrange("p (a c) -> p a c", a=16), src,
                             wait=T.mark())
                T.w(mld)
                T.w(S.mark())
                if mg == 0:
                    T.w(V.mark())
                for m2 in range(2):
                    m = mg * 2 + m2
                    for k in range(16):
                        T.matmul(psA[:, 0:512] if m2 == 0 else psB_[:, 0:512],
                                 upw[mg % 2][:, k * 256 + m2 * 128: k * 256 + m2 * 128 + 128],
                                 x1b[:, k * 512:(k + 1) * 512],
                                 start=(k == 0), stop=(k == 15))
                S.w(T.mark())
                for m2 in range(2):
                    m = mg * 2 + m2
                    S.activation(out=h_sb[:, m * 512:(m + 1) * 512],
                                 in_=psA[:, 0:512] if m2 == 0 else psB_[:, 0:512],
                                 func=AF.Gelu_apprx_tanh, bias=ub[:, m:m + 1])
            for m in range(16):
                src = dn_d[:, m * 128:(m + 1) * 128].rearrange("(a p) c -> p a c", p=128)
                mld = SY.dma(dnw[m % 2][:].rearrange("p (a c) -> p a c", a=64), src,
                             wait=T.mark())
                T.w(mld)
                T.w(S.mark())
                T.w(V.mark())
                for kf in range(64):
                    T.matmul(psA[:, 0:512], dnw[m % 2][:, kf * 128:(kf + 1) * 128],
                             h_sb[:, kf * 512:(kf + 1) * 512],
                             start=(kf == 0), stop=(kf == 63))
                V.w(T.mark())
                V.tensor_scalar(out=z_f[:, m * 512:(m + 1) * 512], in0=psA[:, 0:512],
                                scalar1=db[:, m:m + 1], scalar2=None, op0=ALU.add)
                V.tensor_tensor(out=z_f[:, m * 512:(m + 1) * 512],
                                in0=z_f[:, m * 512:(m + 1) * 512],
                                in1=y_f[:, m * 512:(m + 1) * 512], op=ALU.add)
            ztiles = [z_f[:, k * 512:(k + 1) * 512] for k in range(16)]
            T.w(V.mark())
            for k in range(16):
                T.matmul(psRow[0:1, 0:512], ones_col[:], ztiles[k],
                         start=(k == 0), stop=(k == 15))
            V.w(T.mark())
            V.tensor_scalar(out=rowm[:], in0=psRow[0:1, 0:512],
                            scalar1=_f32(1.0 / 2048.0), scalar2=None, op0=ALU.mult)
            for k in range(16):
                V.tensor_tensor(out=sqb[:], in0=ztiles[k], in1=ztiles[k], op=ALU.mult)
                T.w(V.mark())
                T.matmul(psRow[0:1, 0:512], ones_col[:], sqb[:],
                         start=(k == 0), stop=(k == 15))
                V.w(T.mark())
            V.tensor_scalar(out=rowq[:], in0=psRow[0:1, 0:512],
                            scalar1=_f32(1.0 / 2048.0), scalar2=None, op0=ALU.mult)
            V.tensor_tensor(out=rowt[:], in0=rowm[:], in1=rowm[:], op=ALU.mult)
            V.tensor_tensor(out=rowq[:], in0=rowq[:], in1=rowt[:], op=ALU.subtract)
            V.tensor_scalar(out=rowq[:], in0=rowq[:], scalar1=_f32(EPS_LN),
                            scalar2=None, op0=ALU.add)
            S.w(V.mark())
            S.activation(out=rowt[:], in_=rowq[:], func=AF.Sqrt)
            V.w(S.mark())
            V.reciprocal(out=rowr[:], in_=rowt[:])
            V.tensor_tensor(out=rowt[:], in0=rowr[:], in1=rowr[:], op=ALU.mult)
            V.tensor_tensor(out=rowt[:], in0=rowt[:], in1=rowq[:], op=ALU.mult)
            V.tensor_scalar(out=rowt[:], in0=rowt[:], scalar1=-0.5, scalar2=1.5,
                            op0=ALU.mult, op1=ALU.add)
            V.tensor_tensor(out=rowr[:], in0=rowr[:], in1=rowt[:], op=ALU.mult)
            T.w(V.mark())
            T.matmul(psA[:, 0:512], ones_row[:], rowm[0:1, 0:512], start=True, stop=True)
            T.matmul(psB_[:, 0:512], ones_row[:], rowr[0:1, 0:512], start=True, stop=True)
            V.w(T.mark())
            for k in range(16):
                V.tensor_tensor(out=z_f[:, k * 512:(k + 1) * 512], in0=ztiles[k],
                                in1=psA[:, 0:512], op=ALU.subtract)
                V.tensor_tensor(out=z_f[:, k * 512:(k + 1) * 512], in0=ztiles[k],
                                in1=psB_[:, 0:512], op=ALU.mult)
            for tt in range(4):
                T.w(V.mark())
                for k in range(16):
                    if k > 0 and k % 4 == 0:
                        T.w(V.mark())
                    T.transpose(psT[:, (k % 4) * 128:(k % 4 + 1) * 128],
                                z_f[:, k * 512 + tt * 128: k * 512 + tt * 128 + 128],
                                ident_s[:])
                    if k % 4 == 3:
                        V.w(T.mark())
                        for k4 in range(k - 3, k + 1):
                            V.tensor_copy(out=outb[:, k4 * 128:(k4 + 1) * 128],
                                          in_=psT[:, (k4 % 4) * 128:(k4 % 4 + 1) * 128])
                mo = V.mark()
                GP.dma(y_d[t0 + tt * 128: t0 + (tt + 1) * 128, :], outb[:], wait=mo)
                V.w(GP.mark())
        GP.drain()
        SY.drain()

    run_passes(nc, sems, prog)
    return nc


import ml_dtypes
from concourse.bass_utils import run_bass_kernel_spmd

_CACHE = {}


def _programs():
    if "dq" not in _CACHE:
        _CACHE["dq"] = build_dequant([("up", 256, 8192), ("dn", 1024, 2048)])[0]
        _CACHE["attn"] = build_attn(4096)
        _CACHE["tail"] = build_tail(1024)
    return _CACHE["dq"], _CACHE["attn"], _CACHE["tail"]


def _bf16(a):
    a = np.asarray(a)
    if a.dtype == np.uint16:
        return a.view(ml_dtypes.bfloat16)
    return a.astype(ml_dtypes.bfloat16)


def kernel(L0_attn_x, L0_attn_Wq, L0_attn_Wk, L0_attn_Wv, L0_attn_Wbeta,
           L0_attn_Wg1, L0_attn_Wg2, L0_attn_g_norm, L0_attn_Wo,
           L0_ffn_up_kernel, L0_ffn_up_bias, L0_ffn_down_kernel,
           L0_ffn_down_bias):
    x = np.ascontiguousarray(np.asarray(L0_attn_x, dtype=np.float32))
    Wq = np.ascontiguousarray(np.asarray(L0_attn_Wq, dtype=np.float32))
    Wk = np.ascontiguousarray(np.asarray(L0_attn_Wk, dtype=np.float32))
    Wv = np.ascontiguousarray(np.asarray(L0_attn_Wv, dtype=np.float32))
    Wbeta = np.ascontiguousarray(np.asarray(L0_attn_Wbeta, dtype=np.float32))
    Wg1 = np.ascontiguousarray(np.asarray(L0_attn_Wg1, dtype=np.float32))
    Wg2 = np.ascontiguousarray(np.asarray(L0_attn_Wg2, dtype=np.float32))
    g_norm = np.ascontiguousarray(np.asarray(L0_attn_g_norm, dtype=np.float32))
    Wo = np.ascontiguousarray(np.asarray(L0_attn_Wo, dtype=np.float32))
    up = np.ascontiguousarray(np.asarray(L0_ffn_up_kernel, dtype=np.float32))
    ub = np.ascontiguousarray(np.asarray(L0_ffn_up_bias, dtype=np.float32))
    dn = np.ascontiguousarray(np.asarray(L0_ffn_down_kernel, dtype=np.float32))
    db = np.ascontiguousarray(np.asarray(L0_ffn_down_bias, dtype=np.float32))

    ident = np.eye(128, dtype=np.float32)
    ii, jj = np.indices((128, 128))
    mask_lowneg = np.where(ii > jj, -1.0, 0.0).astype(np.float32)
    mask_triu = np.where(ii <= jj, 1.0, 0.0).astype(ml_dtypes.bfloat16)

    nc_dq, nc_attn, nc_tail = _programs()

    in_maps = []
    for c in range(8):
        in_maps.append({
            "ident": ident,
            "up": np.ascontiguousarray(up[256 * c:256 * (c + 1)]),
            "dn": np.ascontiguousarray(dn[1024 * c:1024 * (c + 1)]),
        })
    res = run_bass_kernel_spmd(nc_dq, in_maps, core_ids=list(range(8)))
    up_dq = np.concatenate([_bf16(res.results[c]["up_dq"]) for c in range(8)], axis=0)
    dn_dq = np.concatenate([_bf16(res.results[c]["dn_dq"]) for c in range(8)], axis=0)

    in_maps = []
    for c in range(8):
        b, a = divmod(c, 4)
        rows = slice(512 * a, 512 * (a + 1))
        in_maps.append({
            "xb": np.ascontiguousarray(x[b]),
            "Wq_s": np.ascontiguousarray(Wq[rows]),
            "Wk_s": np.ascontiguousarray(Wk[rows]),
            "Wv_s": np.ascontiguousarray(Wv[rows]),
            "Wbeta_s": np.ascontiguousarray(Wbeta[4 * a:4 * (a + 1)]),
            "Wg1": Wg1,
            "Wg2_s": np.ascontiguousarray(Wg2[rows]),
            "gnorm_cols": np.ascontiguousarray(g_norm[rows].reshape(4, 128).T),
            "ident": ident,
            "mask_lowneg": mask_lowneg,
            "mask_triu": mask_triu,
        })
    res = run_bass_kernel_spmd(nc_attn, in_maps, core_ids=list(range(8)))
    og = [np.asarray(res.results[c]["og_out"], dtype=np.float32) for c in range(8)]
    og_b = [np.concatenate([og[4 * b + a] for a in range(4)], axis=0) for b in range(2)]

    ub_cols = np.ascontiguousarray(ub.reshape(64, 128).T)
    db_cols = np.ascontiguousarray(db.reshape(16, 128).T)
    in_maps = []
    for c in range(8):
        b, qd = divmod(c, 4)
        tsl = slice(1024 * qd, 1024 * (qd + 1))
        in_maps.append({
            "og_full": np.ascontiguousarray(og_b[b][:, tsl]),
            "x_slice": np.ascontiguousarray(x[b, tsl]),
            "Wo": Wo,
            "up_dq": up_dq,
            "dn_dq": dn_dq,
            "ub_cols": ub_cols,
            "db_cols": db_cols,
            "ident": ident,
        })
    res = run_bass_kernel_spmd(nc_tail, in_maps, core_ids=list(range(8)))
    out = np.empty((2, 4096, 2048), dtype=np.float32)
    for c in range(8):
        b, qd = divmod(c, 4)
        out[b, 1024 * qd:1024 * (qd + 1)] = np.asarray(res.results[c]["y_out"], dtype=np.float32)
    return out
